# version 09
# baseline (speedup 1.0000x reference)
import numpy as np
import ml_dtypes
from contextlib import ExitStack

import concourse.bass as bass
import concourse.mybir as mybir
from concourse.bass_utils import run_bass_kernel_spmd

F32 = mybir.dt.float32
BF16 = mybir.dt.bfloat16
AF = mybir.ActivationFunctionType
ALU = mybir.AluOpType
AX = mybir.AxisListType

D = 1024
S = 2048
NH = 16
DH = 64
H = 2048
NE = 10
NCORE = 8
SCALE = 0.125
EPS = 1e-5
DC = D // 128
TT = S // 512
T128 = S // 128


def _consts():
    bfa = lambda a: np.ascontiguousarray(a).astype(ml_dtypes.bfloat16)
    c = {}
    c["ident"] = np.eye(128, dtype=np.float32)
    c["nident"] = -np.eye(128, dtype=np.float32)
    c["eyeb"] = bfa(np.eye(128, dtype=np.float32))
    inv = 1.0 / (10000.0 ** (np.arange(0, DH, 2, dtype=np.float32) / DH))
    t = np.arange(S, dtype=np.float32)
    fr = np.concatenate([np.outer(t, inv), np.outer(t, inv)], axis=1)
    cfm = np.cos(fr).T
    sfm = np.sin(fr).T
    c["cosb"] = bfa(np.concatenate([cfm, cfm], axis=0))
    c["sinb"] = bfa(np.concatenate([sfm, sfm], axis=0))
    R2 = np.zeros((128, 128), dtype=np.float32)
    for hb in (0, 64):
        for d in range(32):
            R2[hb + d, hb + d + 32] = -1.0
            R2[hb + d + 32, hb + d] = 1.0
    c["R2T"] = bfa(R2.T)
    p = np.arange(128)[:, None]
    cidx = np.arange(896)[None, :]
    c["maskq"] = bfa((cidx >= (p + 384)).astype(np.float32))
    c["onesbr"] = bfa(np.ones((1, 128), dtype=np.float32))
    c["ones32r"] = np.ones((1, 128), dtype=np.float32)
    c["ones32c"] = np.ones((128, 1), dtype=np.float32)
    ey = np.zeros((NE, NE * 128), dtype=np.float32)
    for e in range(NE):
        ey[e, e * 128:(e + 1) * 128] = 1.0
    c["eye10b"] = bfa(ey)
    return c


class Prog:
    ENG_SEM = {"t": "pe", "v": "dv", "s": "ac"}

    def __init__(self, cnt=None, wfloor=None):
        self.ops = {k: [] for k in ("t", "v", "s", "sync", "gp")}
        self.cnt = dict(cnt) if cnt else {k: 0 for k in
                                          ("pe", "dv", "ac", "ld", "st", "wg", "pl")}
        self.wmax = {k: (dict(wfloor) if wfloor else {}) for k in self.ops}

    def op(self, eng, fn, waits=(), inc=None):
        wl = []
        for snm, val in waits:
            if not val or val <= 0:
                continue
            if self.wmax[eng].get(snm, 0) >= val:
                continue
            self.wmax[eng][snm] = val
            wl.append((snm, val))
        icm = None
        if inc is not None:
            snm, k = inc if isinstance(inc, tuple) else (self.ENG_SEM[eng], inc)
            self.cnt[snm] += k
            icm = (snm, k)
        self.ops[eng].append((fn, wl, icm))
        return self.cnt[icm[0]] if icm else None

    def n(self, sem):
        return self.cnt[sem]

    def emit(self, block, sems):
        def mk(eng_key):
            def body(eng):
                for fn, wl, icm in self.ops[eng_key]:
                    for snm, val in wl:
                        eng.wait_ge(sems[snm], val)
                    ins = fn(eng)
                    if icm is not None:
                        ins.then_inc(sems[icm[0]], icm[1])
            return body
        for key, reg in (("sync", block.sync), ("gp", block.gpsimd),
                         ("t", block.tensor), ("v", block.vector),
                         ("s", block.scalar)):
            if self.ops[key]:
                reg(mk(key))


class Bank:
    def __init__(self):
        self.w = []

    def acquire(self):
        w, self.w = self.w, []
        return w

    def read(self, sem, cnt):
        self.w.append((sem, cnt))


def build_nc(stages=("msets", "msetv", "bias", "xin", "qkv", "rope", "vt", "attn", "proj", "ln1", "gate", "ffn")):
    stages = set(stages)
    nc = bass.Bass("TRN2", target_bir_lowering=False,
                   detect_race_conditions=False)

    ap = {}
    def din(name, shape, dt=F32):
        ap[name] = nc.dram_tensor(name, list(shape), dt, kind="ExternalInput").ap()
    din("x", (S, D))
    din("qkv_w", (3 * D, D)); din("qkv_b", (3 * D,))
    din("proj_w", (D, D)); din("proj_b", (D,))
    din("gate_w", (D, 8)); din("gate_b", (8,))
    din("rw1", (8, D, H)); din("rw2", (8, D, H)); din("rw3", (8, H, D))
    din("rb1", (8, H)); din("rb2", (8, H)); din("rb3", (8, D))
    din("sw1", (2, D, H)); din("sw2", (2, D, H)); din("sw3", (2, H, D))
    din("sb1", (2, H)); din("sb2", (2, H)); din("sb3", (2, D))
    for nm, shape, dt in [("ident", (128, 128), F32), ("nident", (128, 128), F32),
                          ("eyeb", (128, 128), BF16),
                          ("cosb", (128, S), BF16), ("sinb", (128, S), BF16),
                          ("R2T", (128, 128), BF16), ("maskq", (128, 896), BF16),
                          ("onesbr", (1, 128), BF16), ("ones32r", (1, 128), F32),
                          ("ones32c", (128, 1), F32),
                          ("eye10b", (NE, NE * 128), BF16)]:
        din(nm, shape, dt)
    y = nc.dram_tensor("y", [S, D], F32, kind="ExternalOutput").ap()

    es = ExitStack()
    sb = lambda nm, shape, dt: es.enter_context(nc.sbuf_tensor(nm, list(shape), dt))

    with es:
        x_fm = sb("x_fm", (128, DC * S), F32)
        x_bf = sb("x_bf", (128, DC * S), BF16)
        ident = sb("s_ident", (128, 128), F32)
        nident = sb("s_nident", (128, 128), F32)
        onesbr = sb("s_onesbr", (1, 128), BF16)
        ones32r = sb("s_ones32r", (1, 128), F32)
        ones32c = sb("s_ones32c", (128, 1), F32)
        qkvb = sb("qkvb", (128, 24), F32)
        projb = sb("projb", (128, 8), F32)
        gateb = sb("gateb", (8, 1), F32)
        b1c = sb("b1c", (128, 160), F32)
        b2c = sb("b2c", (128, 160), F32)
        b3 = sb("b3", (NE, D), BF16)
        gw = sb("gw", (128, 64), F32)
        m_fm = sb("m_fm", (NE, S), BF16)
        lnrow = sb("lnrow", (1, 2048), F32)

        x_fm3 = x_fm[:].rearrange("p (dc t) -> p dc t", dc=DC)
        x_bf3 = x_bf[:].rearrange("p (dc t) -> p dc t", dc=DC)

        sems = {s: es.enter_context(nc.semaphore(s))
                for s in ("ld", "st", "wg", "pe", "dv", "ac", "pl")}

        P = Prog()
        es1 = ExitStack()
        s1 = lambda nm, shape, dt: es1.enter_context(nc.sbuf_tensor(nm, list(shape), dt))
        es_b = ExitStack()
        b1s = es_b.enter_context(nc.sbuf_tensor("b1s", [NE, H], F32, side="right"))
        b2s = es_b.enter_context(nc.sbuf_tensor("b2s", [NE, H], F32, side="right"))
        qb24 = es_b.enter_context(nc.sbuf_tensor("qb24", [24, 128], F32, side="right"))
        pb8 = es_b.enter_context(nc.sbuf_tensor("pb8", [8, 128], F32, side="right"))
        cosb = s1("sb_cosb", (128, S), BF16)
        sinb = s1("sb_sinb", (128, S), BF16)
        R2T = s1("sb_R2T", (128, 128), BF16)
        eyeb = s1("sb_eyeb", (128, 128), BF16)
        maskq = s1("sb_maskq", (128, 896), BF16)
        stage = s1("stage", (128, 2048), F32)
        wqkvT = s1("wqkvT", (128, 2048), BF16)
        qraw = s1("qraw", (128, S), BF16)
        qst = s1("qst", (128, 2 * S), BF16)
        kst = s1("kst", (128, 2 * S), BF16)
        vT = s1("vT", (128, 16 * 132), BF16)
        probsT = s1("probsT", (128, 7 * 512), BF16)
        scr1 = s1("scr1", (128, 512), F32)
        scr2 = s1("scr2", (128, 512), F32)
        mxc = s1("mxc", (128, 48), F32)
        vscr = s1("vscr", (128, 512), F32)
        srow = lnrow[0:1, 0:512]
        rrow = lnrow[0:1, 512:1024]
        lrow = lnrow[0:1, 1024:1536]
        ac_lrow_free = [0]
        sumbc = s1("sumbc", (64, 512), F32)

        vT4 = vT[:].rearrange("p (kt h c) -> p kt h c", kt=16, h=2)
        probsT3 = probsT[:].rearrange("p (sl c) -> p sl c", sl=7)

        S0 = es1.enter_context(nc.psum_tensor("pS0", [128, 512], F32))
        S1b = es1.enter_context(nc.psum_tensor("pS1", [128, 512], F32))
        T0 = es1.enter_context(nc.psum_tensor("pT0", [128, 512], F32))
        T1 = es1.enter_context(nc.psum_tensor("pT1", [128, 512], F32))
        C0 = es1.enter_context(nc.psum_tensor("pC0", [128, 512], F32))
        C1 = es1.enter_context(nc.psum_tensor("pC1", [128, 512], F32))
        RB = es1.enter_context(nc.psum_tensor("pRB", [128, 512], F32))
        vtp = es1.enter_context(nc.psum_tensor("vtp", [128, 512], F32))

        bk = {t_: Bank() for t_ in ("S0", "S1", "T0", "T1", "C0", "C1", "RB", "VT")}
        PSB = {"S0": S0, "S1": S1b, "T0": T0, "T1": T1, "C0": C0, "C1": C1, "RB": RB}

        pend = [0]
        def tguard():
            pend[0] = P.n("pe")

        def top(fn, waits=(), inc=None):
            w = list(waits)
            if pend[0]:
                w.append(("pe", pend[0]))
                pend[0] = 0
            return P.op("t", fn, waits=w, inc=inc)

        def ld(dst_ap, src_ap, gpq=False, extra=()):
            eng, semn = ("gp", "wg") if gpq else ("sync", "ld")
            P.op(eng, lambda e, d=dst_ap, s=src_ap: e.dma_start(out=d, in_=s),
                 waits=extra, inc=(semn, 16))
            return P.n(semn)

        for dst, src in [(ident[:], ap["ident"][:]), (nident[:], ap["nident"][:]),
                         (onesbr[:], ap["onesbr"][:]),
                         (ones32r[:], ap["ones32r"][:]), (ones32c[:], ap["ones32c"][:]),
                         (cosb[:], ap["cosb"][:]), (sinb[:], ap["sinb"][:]),
                         (R2T[:], ap["R2T"][:]), (eyeb[:], ap["eyeb"][:]),
                         (maskq[:], ap["maskq"][:]),
                         (qb24[:], ap["qkv_b"][:].rearrange("(c p) -> c p", c=24)),
                         (pb8[:], ap["proj_b"][:].rearrange("(c p) -> c p", c=8)),
                         (gateb[:], ap["gate_b"][:].unsqueeze(1)),
                         (b1s[0:8, :], ap["rb1"][:]), (b1s[8:10, :], ap["sb1"][:]),
                         (b2s[0:8, :], ap["rb2"][:]), (b2s[8:10, :], ap["sb2"][:])]:
            ld(dst, src)
        ld(gw[:].rearrange("p (dc e) -> p dc e", dc=DC),
           ap["gate_w"][:].rearrange("(dc p) e -> p dc e", p=128))
        n_ld0 = P.n("ld")

        dv_mfm_init = dv_vT_ones = 0
        if "msets" in stages:
            P.op("v", lambda v: v.memset(m_fm[:], 0.5), inc=1)
            dv_mfm_init = P.n("dv")
        if "msetv" in stages:
            P.op("v", lambda v: v.memset(vT[:], 1.0), inc=1)
            P.op("v", lambda v: v.memset(kst[:], 1.0), inc=1)
            dv_vT_ones = P.n("dv")

        wld = [("ld", n_ld0)]
        if "bias" not in stages:
            es_b.close()
            ctx = s1("ctx", (128, DC * S), BF16)
            ctx3 = ctx[:].rearrange("p (dc t) -> p dc t", dc=DC)
        if "bias" in stages:
          tguard()
          top(lambda t: t.transpose(S0[:, 0:24], qb24[:], ident[0:24, 0:24]),
            waits=wld + bk["S0"].acquire(), inc=1)
          tguard()
          top(lambda t: t.transpose(S0[:, 24:32], pb8[:], ident[0:8, 0:8]), inc=1)
          pe_bt = P.n("pe")
          P.op("v", lambda v: v.tensor_copy(qkvb[:], S0[:, 0:24]),
             waits=[("pe", pe_bt)], inc=1)
          P.op("v", lambda v: v.tensor_copy(projb[:], S0[:, 24:32]), inc=1)
          bk["S0"].read("dv", P.n("dv"))
          for hc in range(16):
            tguard()
            top(lambda t, hc=hc: t.transpose(S1b[:, hc * 10:hc * 10 + 10],
                                             b1s[0:NE, hc * 128:(hc + 1) * 128],
                                             ident[0:NE, 0:NE]),
                waits=(wld + bk["S1"].acquire()) if hc == 0 else wld, inc=1)
            tguard()
            top(lambda t, hc=hc: t.transpose(T0[:, hc * 10:hc * 10 + 10],
                                             b2s[0:NE, hc * 128:(hc + 1) * 128],
                                             ident[0:NE, 0:NE]),
                waits=bk["T0"].acquire() if hc == 0 else (), inc=1)
          pe_b12 = P.n("pe")
          P.op("v", lambda v: v.tensor_copy(b1c[:], S1b[:, 0:160]),
             waits=[("pe", pe_b12)], inc=1)
          P.op("v", lambda v: v.tensor_copy(b2c[:], T0[:, 0:160]), inc=1)
          bk["S1"].read("dv", P.n("dv"))
          bk["T0"].read("dv", P.n("dv"))
          tguard()
          es_b.close()
          ctx = s1("ctx", (128, DC * S), BF16)
          ctx3 = ctx[:].rearrange("p (dc t) -> p dc t", dc=DC)

        stage_free = [[], []]

        def load_stage(par, src_ap):
            nld = ld(stage[:, par * 1024:(par + 1) * 1024], src_ap,
                     extra=stage_free[par])
            stage_free[par] = []
            return nld

        def transpose_chunk(par, nld, banks, evac):
            for dcq in range(2):
                bnm = banks[dcq]
                bank = PSB[bnm]
                for j in range(4):
                    dc = dcq * 4 + j
                    w = [("ld", nld)]
                    if j == 0:
                        w += bk[bnm].acquire()
                    top(lambda t, bank=bank, j=j, par=par, dc=dc:
                        t.transpose(bank[:, j * 128:(j + 1) * 128],
                                    stage[:, par * 1024 + dc * 128:
                                          par * 1024 + (dc + 1) * 128],
                                    ident[:]),
                        waits=w, inc=(1 if dc == 7 else None))
            stage_free[par] = [("pe", P.n("pe"))]
            pe_tp = P.n("pe")
            for dcq in range(2):
                evac(banks[dcq], dcq, pe_tp)

        for i16 in range(T128 if "xin" in stages else 0):
            par = i16 % 2
            nld = load_stage(par, ap["x"][i16 * 128:(i16 + 1) * 128, :])
            banks = ("T0", "T1") if i16 % 2 == 0 else ("C0", "C1")

            def evac_x(bnm, dcq, pe_tp, i16=i16):
                bank = PSB[bnm]
                src = bank[:].rearrange("p (j c) -> p j c", j=4)
                P.op("v", lambda v, src=src, dcq=dcq, i16=i16:
                     v.tensor_copy(x_fm3[:, dcq * 4:(dcq + 1) * 4,
                                         i16 * 128:(i16 + 1) * 128], src),
                     waits=[("pe", pe_tp)], inc=1)
                dv_cp = P.n("dv")
                bk[bnm].read("dv", dv_cp)
                P.op("s", lambda s, dcq=dcq, i16=i16:
                     s.copy(x_bf3[:, dcq * 4:(dcq + 1) * 4,
                                  i16 * 128:(i16 + 1) * 128],
                            x_fm3[:, dcq * 4:(dcq + 1) * 4,
                                  i16 * 128:(i16 + 1) * 128]),
                     waits=[("dv", dv_cp)], inc=1)
            transpose_chunk(par, nld, banks, evac_x)

        wdst_free = {"wq": [[], []]}

        def stream_wchunk(src_rows, dstbuf, par, banks, key):
            nld = load_stage(par, src_rows)

            def evac_w(bnm, dcq, pe_tp):
                bank = PSB[bnm]
                P.op("s", lambda s, bank=bank, dcq=dcq:
                     s.copy(dstbuf[:, par * 1024 + dcq * 512:
                                   par * 1024 + (dcq + 1) * 512], bank[:]),
                     waits=[("pe", pe_tp)] + wdst_free[key][par], inc=1)
                bk[bnm].read("ac", P.n("ac"))
            transpose_chunk(par, nld, banks, evac_w)
            wdst_free[key][par] = []
            return P.n("ac")

        vtp_slot_free = [[] for _ in range(4)]
        raw_free = [[]]
        scr2_free = [[]]
        buf_free = [[]]
        chunk_i = [0]
        unit_i = [0]
        qk_ready = [0]

        def qkv_chunk(c, kind):
            par = chunk_i[0] % 2
            banks = ("T0", "T1") if chunk_i[0] % 2 == 0 else ("C0", "C1")
            wrdy = stream_wchunk(ap["qkv_w"][c * 128:(c + 1) * 128, :],
                                 wqkvT, par, banks, "wq")
            mmbanks = ("S0", "S1")
            for tt in range(TT):
                bnm = mmbanks[tt % 2]
                bank = PSB[bnm]
                for dc in range(DC):
                    w = [("ac", wrdy)]
                    if dc == 0:
                        w += bk[bnm].acquire()
                    top(lambda t, bank=bank, par=par, dc=dc, tt=tt:
                        t.matmul(bank[:],
                                 wqkvT[:, par * 1024 + dc * 128:
                                       par * 1024 + (dc + 1) * 128],
                                 x_bf3[:, dc, tt * 512:(tt + 1) * 512],
                                 start=(dc == 0), stop=(dc == DC - 1)),
                        waits=w, inc=(1 if dc == DC - 1 else None))
                pe_mm = P.n("pe")
                evac_dst = (qraw[:, tt * 512:(tt + 1) * 512] if kind != "v"
                            else vscr[:])
                P.op("s", lambda s, bank=bank, c=c, d=evac_dst:
                     s.activation(d, bank[:], AF.Identity,
                                  bias=qkvb[:, c:c + 1], scale=1.0),
                     waits=[("pe", pe_mm)] + raw_free[0], inc=1)
                raw_free[0] = []
                ac_evac = P.n("ac")
                bk[bnm].read("ac", ac_evac)
                if kind in ("q", "k") and "rope" in stages:
                    dstbuf = qst if kind == "q" else kst
                    tguard()
                    top(lambda t, tt=tt: t.matmul(RB[:], R2T[:],
                                                  qraw[:, tt * 512:(tt + 1) * 512],
                                                  start=True, stop=True),
                        waits=[("ac", ac_evac)] + bk["RB"].acquire(), inc=1)
                    pe_rot = P.n("pe")
                    P.op("v", lambda v, tt=tt: v.tensor_tensor(
                        scr1[:], RB[:], sinb[:, tt * 512:(tt + 1) * 512], ALU.mult),
                        waits=[("pe", pe_rot)], inc=1)
                    bk["RB"].read("dv", P.n("dv"))
                    P.op("v", lambda v, tt=tt: v.tensor_tensor(
                        scr2[:], qraw[:, tt * 512:(tt + 1) * 512],
                        cosb[:, tt * 512:(tt + 1) * 512], ALU.mult),
                        waits=scr2_free[0], inc=1)
                    wg_cos = ("dv", P.n("dv"))
                    for h2 in range(2):
                        P.op("v", lambda v, tt=tt, dstbuf=dstbuf, h2=h2:
                             v.tensor_tensor(
                                 dstbuf[0:64, h2 * S + tt * 512:
                                        h2 * S + (tt + 1) * 512],
                                 scr1[h2 * 64:h2 * 64 + 64, :],
                                 scr2[h2 * 64:h2 * 64 + 64, :], ALU.add),
                             waits=buf_free[0] + [wg_cos], inc=1)
                        buf_free[0] = []
                    scr2_free[0] = [("dv", P.n("dv"))]
                    raw_free[0] = [("dv", P.n("dv")), ("pe", pe_rot)]
                elif kind == "v" and "vt" in stages:
                    for j in range(4):
                        w = [("ac", ac_evac)] + vtp_slot_free[0]
                        vtp_slot_free[0] = []
                        top(lambda t, j=j:
                            t.transpose(vtp[:, j * 128:(j + 1) * 128],
                                        vscr[:, j * 128:(j + 1) * 128],
                                        ident[:]),
                            waits=w, inc=(1 if j == 3 else None))
                    pe_v = P.n("pe")
                    for j in range(4):
                        i16 = tt * 4 + j
                        for hh2 in range(2):
                            P.op("v", lambda v, i16=i16, j=j, hh2=hh2:
                                 v.tensor_copy(
                                     vT4[:, i16, hh2, 0:64],
                                     vtp[:, j * 128 + hh2 * 64:
                                         j * 128 + hh2 * 64 + 64]),
                                 waits=[("pe", pe_v), ("dv", dv_vT_ones)]
                                 + buf_free[0], inc=(1 if (j == 3 and hh2 == 1)
                                                     else None))
                            buf_free[0] = []
                    vtp_slot_free[0] = [("dv", P.n("dv"))]
                    raw_free[0] = [("pe", P.n("pe")), ("dv", P.n("dv"))]
            chunk_i[0] += 1

        slot_free = [0] * 7
        stA = {}

        aidx = [0]

        def stageA(h, si):
            h2 = h % 2
            qc0 = si * 512
            mpar = 24 * (aidx[0] % 2)
            aidx[0] += 1
            for qt in range(4):
                qtg = 4 * si + qt
                ktot = (qtg + 1) * 128
                nks = (ktot + 511) // 512
                tguard()
                for ks in range(nks):
                    kw = min(512, ktot - ks * 512)
                    bnm = ("S0", "S1")[ks % 2]
                    bank = PSB[bnm]
                    w = bk[bnm].acquire() + [("dv", qk_ready[0])]
                    top(lambda t, bank=bank, kw=kw, ks=ks, qtg=qtg, h2=h2:
                        t.matmul(bank[:, 0:kw],
                                 qst[0:64, h2 * S + qtg * 128:
                                     h2 * S + (qtg + 1) * 128],
                                 kst[0:64, h2 * S + ks * 512:
                                     h2 * S + ks * 512 + kw],
                                 start=True, stop=True),
                        waits=w, inc=1)
                    pe_sc = P.n("pe")
                    P.op("v", lambda v, bank=bank, kw=kw, col=mpar + qt * 4 + ks:
                         v.reduce_max(mxc[:, col:col + 1], bank[:, 0:kw],
                                      axis=AX.X),
                         waits=[("pe", pe_sc)], inc=1)
                    bk[bnm].read("dv", P.n("dv"))
                dvr = P.n("dv")
                if nks > 1:
                    P.op("v", lambda v, qt=qt, nks=nks, mp=mpar:
                         v.reduce_max(mxc[:, mp + 16 + qt:mp + 17 + qt],
                                      mxc[:, mp + qt * 4:mp + qt * 4 + nks],
                                      axis=AX.X),
                         waits=[("dv", dvr)], inc=1)
                    dvr2 = P.n("dv")
                    P.op("v", lambda v, qt=qt, mp=mpar: v.tensor_scalar_mul(
                        mxc[:, mp + 20 + qt:mp + 21 + qt],
                        mxc[:, mp + 16 + qt:mp + 17 + qt], -1.0),
                        waits=[("dv", dvr2)], inc=1)
                else:
                    P.op("v", lambda v, qt=qt, mp=mpar: v.tensor_scalar_mul(
                        mxc[:, mp + 20 + qt:mp + 21 + qt],
                        mxc[:, mp + qt * 4:mp + qt * 4 + 1], -1.0),
                        waits=[("dv", dvr)], inc=1)
            dv_mx = P.n("dv")
            tguard()
            for qt in range(4):
                w = [("dv", dv_mx)]
                if qt == 0:
                    w += bk["S0"].acquire()
                top(lambda t, qt=qt, mp=mpar:
                    t.transpose(S0[0:1, qt * 128:(qt + 1) * 128],
                                mxc[:, mp + 20 + qt:mp + 21 + qt],
                                ident[:]),
                    waits=w, inc=(1 if qt == 3 else None))
            pe_m1 = P.n("pe")
            P.op("v", lambda v, h2=h2, qc0=qc0:
                 v.tensor_copy(qst[64:65, h2 * S + qc0:h2 * S + qc0 + 512],
                               S0[0:1, 0:512]),
                 waits=[("pe", pe_m1)], inc=1)
            dv_m1 = P.n("dv")
            bk["S0"].read("dv", dv_m1)
            stA[(h, si)] = dv_m1

        def stageCD(h, si):
            h2 = h % 2
            hp = h // 2
            ktmax = 4 * si + 4
            u = unit_i[0]
            unit_i[0] += 1
            qc0 = si * 512
            dv_m1 = stA[(h, si)]
            ac_exp = {}
            dv_msk = {}
            for kt in range(ktmax):
                bnm = ("T0", "T1")[kt % 2]
                bank = PSB[bnm]
                slot = kt % 7
                tguard()
                w = [("dv", dv_m1), ("pe", slot_free[slot])] + bk[bnm].acquire()
                top(lambda t, bank=bank, kt=kt, h2=h2, qc0=qc0:
                    t.matmul(bank[:],
                             kst[0:65, h2 * S + kt * 128:h2 * S + (kt + 1) * 128],
                             qst[0:65, h2 * S + qc0:h2 * S + qc0 + 512],
                             start=True, stop=True),
                    waits=w, inc=1)
                pe_st = P.n("pe")
                P.op("s", lambda s, bank=bank, slot=slot:
                     s.activation(probsT3[:, slot, :], bank[:], AF.Exp,
                                  bias=0.0, scale=SCALE),
                     waits=[("pe", pe_st)], inc=1)
                bk[bnm].read("ac", P.n("ac"))
                ac_exp[kt] = P.n("ac")
                if kt >= 4 * si:
                    o = kt - 4 * si
                    P.op("v", lambda v, slot=slot, o=o:
                         v.tensor_tensor(probsT3[:, slot, :], probsT3[:, slot, :],
                                         maskq[:, (3 - o) * 128:
                                               (3 - o) * 128 + 512],
                                         ALU.mult),
                         waits=[("ac", ac_exp[kt])], inc=1)
                    dv_msk[kt] = ("dv", P.n("dv"))
            cnm = ("C0", "C1")[u % 2]
            Cb = PSB[cnm]
            tguard()
            for kt in range(ktmax):
                w = [("ac", ac_exp[kt])]
                if kt in dv_msk:
                    w.append(dv_msk[kt])
                if kt == 0:
                    w += bk[cnm].acquire()
                top(lambda t, Cb=Cb, kt=kt, h=h:
                    t.matmul(Cb[0:65, :], vT4[:, kt, h % 2, 0:65],
                             probsT3[:, kt % 7, :],
                             start=(kt == 0), stop=(kt == ktmax - 1)),
                    waits=w, inc=(1 if kt == ktmax - 1 else None))
            pe_ctx = P.n("pe")
            for kt in range(ktmax):
                slot_free[kt % 7] = pe_ctx
            P.op("v", lambda v, Cb=Cb: v.tensor_copy(srow, Cb[64:65, 0:512]),
                 waits=[("pe", pe_ctx)], inc=1)
            dv_srow = P.n("dv")
            P.op("s", lambda s: s.activation(lrow, srow, AF.Ln,
                                             bias=0.0, scale=1.0),
                 waits=[("dv", dv_srow), ("ac", ac_lrow_free[0])], inc=1)
            ac_ln = P.n("ac")
            P.op("s", lambda s: s.activation(rrow, lrow, AF.Exp,
                                             bias=0.0, scale=-1.0),
                 waits=[("ac", ac_ln)], inc=1)
            dv_rr = ("ac", P.n("ac"))
            ac_lrow_free[0] = P.n("ac")
            tguard()
            top(lambda t: t.matmul(RB[0:64, :], ones32r[0:1, 0:64], rrow,
                                   start=True, stop=True),
                waits=[dv_rr] + bk["RB"].acquire(), inc=1)
            pe_rbc = P.n("pe")
            P.op("v", lambda v: v.tensor_copy(sumbc[:], RB[0:64, :]),
                 waits=[("pe", pe_rbc)], inc=1)
            bk["RB"].read("dv", P.n("dv"))
            dv_sbc = P.n("dv")
            P.op("v", lambda v, Cb=Cb, hp=hp, h=h, qc0=qc0:
                 v.tensor_tensor(ctx3[(h % 2) * 64:(h % 2) * 64 + 64, hp,
                                      qc0:qc0 + 512],
                                 Cb[0:64, :], sumbc[:], ALU.mult),
                 waits=[("dv", dv_sbc)], inc=1)
            bk[cnm].read("dv", P.n("dv"))

        for hp in range(8 if "qkv" in stages else 0):
            qkv_chunk(hp, "q")
            qkv_chunk(8 + hp, "k")
            qkv_chunk(16 + hp, "v")
            qk_ready[0] = P.n("dv")
            if "attn" in stages:
                units = [(h, si) for h in (2 * hp, 2 * hp + 1)
                         for si in range(4)]
                stageA(*units[0])
                stageA(*units[1])
                for i in range(len(units)):
                    if i + 2 < len(units):
                        stageA(*units[i + 2])
                    stageCD(*units[i])
            buf_free[0] = [("pe", P.n("pe"))]

        dv_attn = P.n("dv")
        ac_attn = P.n("ac")

        for dco in range(DC if "proj" in stages else 0):
            par = dco % 2
            banks = ("T0", "T1") if dco % 2 == 0 else ("C0", "C1")
            wrdy = stream_wchunk(ap["proj_w"][dco * 128:(dco + 1) * 128, :],
                                 wqkvT, par, banks, "wq")
            for tt in range(TT):
                bnm = ("S0", "S1")[tt % 2]
                bank = PSB[bnm]
                for dci in range(DC):
                    w = [("ac", wrdy), ("dv", dv_attn)]
                    if dci == 0:
                        w += bk[bnm].acquire()
                    top(lambda t, bank=bank, par=par, dci=dci, tt=tt:
                        t.matmul(bank[:],
                                 wqkvT[:, par * 1024 + dci * 128:
                                       par * 1024 + (dci + 1) * 128],
                                 ctx3[:, dci, tt * 512:(tt + 1) * 512],
                                 start=(dci == 0), stop=(dci == DC - 1)),
                        waits=w, inc=(1 if dci == DC - 1 else None))
                pe_p = P.n("pe")
                P.op("v", lambda v, bank=bank, dco=dco, tt=tt:
                     v.scalar_tensor_tensor(x_fm3[:, dco, tt * 512:(tt + 1) * 512],
                                            bank[:], projb[:, dco:dco + 1],
                                            x_fm3[:, dco, tt * 512:(tt + 1) * 512],
                                            ALU.add, ALU.add),
                     waits=[("pe", pe_p)], inc=1)
                bk[bnm].read("dv", P.n("dv"))
            wdst_free["wq"][par] = [("pe", P.n("pe"))]
        dv_proj = P.n("dv")

        scr_pe_free = [0, 0]
        for tt in range(TT if "ln1" in stages else 0):
            for dc in range(DC):
                w = [("dv", dv_proj)]
                if dc == 0:
                    w += bk["S0"].acquire()
                    tguard()
                top(lambda t, dc=dc, tt=tt:
                    t.matmul(S0[0:1, :], ones32c[:],
                             x_fm3[:, dc, tt * 512:(tt + 1) * 512],
                             start=(dc == 0), stop=(dc == DC - 1)),
                    waits=w, inc=(1 if dc == DC - 1 else None))
            for dc in range(DC):
                scr = (scr1, scr2)[dc % 2]
                P.op("s", lambda s, dc=dc, tt=tt, scr=scr:
                     s.activation(scr[:], x_fm3[:, dc, tt * 512:(tt + 1) * 512],
                                  AF.Square, bias=0.0, scale=1.0),
                     waits=[("dv", P.n("dv")), ("pe", scr_pe_free[dc % 2])], inc=1)
                ac_sq = P.n("ac")
                w = [("ac", ac_sq)]
                if dc == 0:
                    w += bk["S1"].acquire()
                top(lambda t, dc=dc, scr=scr:
                    t.matmul(S1b[0:1, :], ones32c[:], scr[:],
                             start=(dc == 0), stop=(dc == DC - 1)),
                    waits=w, inc=1)
                scr_pe_free[dc % 2] = P.n("pe")
            pe_sums = P.n("pe")
            L = lnrow
            P.op("v", lambda v: v.tensor_scalar_mul(L[0:1, 0:512], S0[0:1, :],
                                                    1.0 / D),
                 waits=[("pe", pe_sums)], inc=1)
            P.op("v", lambda v: v.tensor_scalar_mul(L[0:1, 1536:2048], S1b[0:1, :],
                                                    1.0 / D), inc=1)
            bk["S0"].read("dv", P.n("dv"))
            bk["S1"].read("dv", P.n("dv"))
            dvc = P.n("dv")
            P.op("v", lambda v: v.tensor_tensor(L[0:1, 1024:1536], L[0:1, 0:512],
                                                L[0:1, 0:512], ALU.mult),
                 waits=[("dv", dvc)], inc=1)
            P.op("v", lambda v: v.tensor_tensor(L[0:1, 1024:1536],
                                                L[0:1, 1536:2048],
                                                L[0:1, 1024:1536], ALU.subtract),
                 inc=1)
            P.op("v", lambda v: v.tensor_scalar_add(L[0:1, 1024:1536],
                                                    L[0:1, 1024:1536], EPS), inc=1)
            dvv = P.n("dv")
            P.op("s", lambda s: s.activation(L[0:1, 1536:2048], L[0:1, 1024:1536],
                                             AF.Sqrt, bias=0.0, scale=1.0),
                 waits=[("dv", dvv)], inc=1)
            acs = P.n("ac")
            P.op("v", lambda v: v.reciprocal(L[0:1, 512:1024], L[0:1, 1536:2048]),
                 waits=[("ac", acs)], inc=1)
            dvr = P.n("dv")
            P.op("v", lambda v: v.tensor_tensor(L[0:1, 1024:1536], L[0:1, 0:512],
                                                L[0:1, 512:1024], ALU.mult),
                 waits=[("dv", dvr)], inc=1)
            dvb0 = P.n("dv")
            P.op("v", lambda v: v.tensor_scalar_mul(L[0:1, 1024:1536],
                                                    L[0:1, 1024:1536], -1.0),
                 waits=[("dv", dvb0)], inc=1)
            dvab = P.n("dv")
            tguard()
            top(lambda t: t.matmul(T0[:], ones32r[:], L[0:1, 512:1024],
                                   start=True, stop=True),
                waits=[("dv", dvab)] + bk["T0"].acquire(), inc=1)
            tguard()
            top(lambda t: t.matmul(T1[:], ones32r[:], L[0:1, 1024:1536],
                                   start=True, stop=True),
                waits=bk["T1"].acquire(), inc=1)
            pe_bc = P.n("pe")
            for dc in range(DC):
                scr = (scr1, scr2)[dc % 2]
                P.op("v", lambda v, dc=dc, tt=tt, scr=scr:
                     v.tensor_tensor(scr[:], x_fm3[:, dc, tt * 512:(tt + 1) * 512],
                                     T0[:], ALU.mult),
                     waits=[("pe", pe_bc), ("ac", P.n("ac"))], inc=1)
                P.op("v", lambda v, dc=dc, tt=tt, scr=scr:
                     v.tensor_tensor(x_fm3[:, dc, tt * 512:(tt + 1) * 512],
                                     scr[:], T1[:], ALU.add), inc=1)
                dvxn = P.n("dv")
                P.op("s", lambda s, dc=dc, tt=tt:
                     s.copy(x_bf3[:, dc, tt * 512:(tt + 1) * 512],
                            x_fm3[:, dc, tt * 512:(tt + 1) * 512]),
                     waits=[("dv", dvxn)], inc=1)
            bk["T0"].read("dv", P.n("dv"))
            bk["T1"].read("dv", P.n("dv"))
        dv_ln1 = P.n("dv")
        ac_ln1 = P.n("ac")

        dv_mfm = 0

        with nc.Block() as blk1:
            P.emit(blk1, sems)
        es1.close()

        P2 = Prog(cnt=P.cnt, wfloor=P.cnt)
        pend2 = [0]

        def tguard2():
            pend2[0] = P2.n("pe")

        def top2(fn, waits=(), inc=None):
            w = list(waits)
            if pend2[0]:
                w.append(("pe", pend2[0]))
                pend2[0] = 0
            return P2.op("t", fn, waits=w, inc=inc)

        def ld2(dst_ap, src_ap, gpq=False, extra=()):
            eng, semn = ("gp", "wg") if gpq else ("sync", "ld")
            P2.op(eng, lambda e, d=dst_ap, s=src_ap: e.dma_start(out=d, in_=s),
                  waits=extra, inc=(semn, 16))
            return P2.n(semn)

        es2 = ExitStack()
        s2 = lambda nm, shape, dt: es2.enter_context(
            nc.sbuf_tensor(nm, list(shape), dt))
        glu = s2("glu", (128, 8 * S), BF16)
        w1b = s2("w1b", (128, 2048), BF16)
        w2b = s2("w2b", (128, 2048), BF16)
        w3r = s2("w3r", (128, 8 * 1024), BF16)
        m_bc = s2("m_bc", (128, S), BF16)
        eye10 = s2("eye10", (NE, NE * 128), BF16)
        fsc1 = s2("fsc1", (128, 512), F32)
        fsc2 = s2("fsc2", (128, 512), F32)
        res_tm = s2("res_tm", (128, 2048), F32)
        out_tm = s2("out_tm", (128, 2048), F32)
        sqt = s2("sqt", (128, 512), F32)
        lnc = s2("lnc", (128, 8), F32)
        g_s = s2("g_s", (8, S), F32)
        gtm = s2("gtm", (128, 8), F32)
        gsc = s2("gsc", (128, 48), F32)
        glu3 = glu[:].rearrange("p (hc t) -> p hc t", hc=8)
        w3r3 = w3r[:].rearrange("p (hc d) -> p hc d", hc=8)

        pU = [es2.enter_context(nc.psum_tensor(f"pu{i}", [128, 512], F32))
              for i in range(8)]
        U1a, U1b, U2a, U2b, Da, Db, MBp, LBp = pU
        bk2 = {t_: Bank() for t_ in ("U1a", "U1b", "U2a", "U2b", "Da", "Db",
                                     "MB", "LB")}
        PSB2 = {"U1a": U1a, "U1b": U1b, "U2a": U2a, "U2b": U2b,
                "Da": Da, "Db": Db, "MB": MBp, "LB": LBp}

        n_eye = ld2(eye10[:], ap["eye10b"][:])
        ld2(b3[0:8, :], ap["rb3"][:], gpq=True)
        n_b3 = ld2(b3[8:10, :], ap["sb3"][:], gpq=True)

        for tt in range(TT if "gate" in stages else 0):
            for dc in range(DC):
                w = [("ac", ac_ln1), ("dv", dv_ln1)]
                if dc == 0:
                    w += bk2["Da"].acquire()
                top2(lambda t, dc=dc, tt=tt:
                    t.matmul(Da[0:8, :], gw[:, dc * 8:(dc + 1) * 8],
                             x_fm3[:, dc, tt * 512:(tt + 1) * 512],
                             start=(dc == 0), stop=(dc == DC - 1)),
                    waits=w, inc=(1 if dc == DC - 1 else None))
            pe_g = P2.n("pe")
            P2.op("s", lambda s, tt=tt:
                 s.activation(g_s[0:8, tt * 512:(tt + 1) * 512], Da[0:8, :],
                              AF.Identity, bias=gateb[:, 0:1], scale=1.0),
                 waits=[("pe", pe_g)], inc=1)
            bk2["Da"].read("ac", P2.n("ac"))
        ac_gs = P2.n("ac")
        for i16 in range(T128 if "gate" in stages else 0):
            tguard2()
            top2(lambda t, i16=i16:
                t.transpose(Db[:, 0:8], g_s[0:8, i16 * 128:(i16 + 1) * 128],
                            ident[0:8, 0:8]),
                waits=[("ac", ac_gs)] + bk2["Db"].acquire(), inc=1)
            pe_gt = P2.n("pe")
            G = gsc
            P2.op("v", lambda v: v.tensor_copy(gtm[:], Db[:, 0:8]),
                 waits=[("pe", pe_gt)], inc=1)
            bk2["Db"].read("dv", P2.n("dv"))
            dv0 = P2.n("dv")
            P2.op("v", lambda v: v.reduce_max(G[:, 8:9], gtm[:], axis=AX.X),
                 waits=[("dv", dv0)], inc=1)
            dvA = P2.n("dv")
            P2.op("v", lambda v: v.tensor_scalar_mul(G[:, 9:10], G[:, 8:9], -1.0),
                 waits=[("dv", dvA)], inc=1)
            dvB = P2.n("dv")
            P2.op("s", lambda s: s.activation(G[:, 0:8], gtm[:], AF.Exp,
                                             bias=G[:, 9:10], scale=1.0),
                 waits=[("dv", dvB)], inc=1)
            acE = P2.n("ac")
            P2.op("v", lambda v: v.reduce_sum(G[:, 10:11], G[:, 0:8], axis=AX.X),
                 waits=[("ac", acE)], inc=1)
            dvS = P2.n("dv")
            P2.op("v", lambda v: v.reciprocal(G[:, 11:12], G[:, 10:11]),
                 waits=[("dv", dvS)], inc=1)
            dvR = P2.n("dv")
            P2.op("v", lambda v: v.tensor_scalar_mul(G[:, 16:24], G[:, 0:8],
                                                    G[:, 11:12]),
                 waits=[("dv", dvR)], inc=1)
            dvP = P2.n("dv")
            P2.op("v", lambda v: v.reduce_max(G[:, 12:13], G[:, 16:24], axis=AX.X),
                 waits=[("dv", dvP)], inc=1)
            dvM1 = P2.n("dv")
            P2.op("v", lambda v: v.tensor_scalar(G[:, 24:32], G[:, 16:24],
                                                G[:, 12:13], None, ALU.subtract),
                 waits=[("dv", dvM1)], inc=1)
            dvD = P2.n("dv")
            P2.op("v", lambda v: v.tensor_single_scalar(G[:, 32:40], G[:, 24:32],
                                                       0.0, ALU.is_equal),
                 waits=[("dv", dvD)], inc=1)
            dvEQ = P2.n("dv")
            P2.op("v", lambda v: v.scalar_tensor_tensor(G[:, 24:32], G[:, 32:40],
                                                       -1e30, G[:, 16:24],
                                                       ALU.mult, ALU.add),
                 waits=[("dv", dvEQ)], inc=1)
            dvMK = P2.n("dv")
            P2.op("v", lambda v: v.reduce_max(G[:, 13:14], G[:, 24:32], axis=AX.X),
                 waits=[("dv", dvMK)], inc=1)
            dvM2 = P2.n("dv")
            P2.op("v", lambda v: v.tensor_single_scalar(G[:, 32:40], G[:, 16:24],
                                                       G[:, 13:14], ALU.is_ge),
                 waits=[("dv", dvM2)], inc=1)
            dvKP = P2.n("dv")
            P2.op("v", lambda v: v.tensor_tensor(G[:, 40:48], G[:, 16:24],
                                                G[:, 32:40], ALU.mult),
                 waits=[("dv", dvKP)], inc=1)
            dvMW = P2.n("dv")
            tguard2()
            top2(lambda t: t.transpose(Db[0:8, 128:256], G[:, 40:48], ident[:]),
                waits=[("dv", dvMW)] + bk2["Db"].acquire(), inc=1)
            pe_mw = P2.n("pe")
            P2.op("v", lambda v, i16=i16:
                 v.tensor_copy(m_fm[0:8, i16 * 128:(i16 + 1) * 128],
                               Db[0:8, 128:256]),
                 waits=[("pe", pe_mw), ("dv", dv_mfm_init)], inc=1)
            bk2["Db"].read("dv", P2.n("dv"))
        dv_mfm = P2.n("dv")



        def w12_src(e, hc):
            wt1 = ap["rw1"] if e < 8 else ap["sw1"]
            wt2 = ap["rw2"] if e < 8 else ap["sw2"]
            ei = e % 8
            s1_ = wt1[ei, :, hc * 128:(hc + 1) * 128].rearrange(
                "(dc p) h -> p dc h", p=128)
            s2_ = wt2[ei, :, hc * 128:(hc + 1) * 128].rearrange(
                "(dc p) h -> p dc h", p=128)
            return s1_, s2_

        def w3_src(e, hc):
            wt3 = ap["rw3"] if e < 8 else ap["sw3"]
            return wt3[e % 8, hc * 128:(hc + 1) * 128, :]

        EORDER = [8, 9] + list(range(8))
        wslices = [(e, hc) for e in EORDER for hc in range(16)]
        w12v = lambda buf, par: buf[:, par * 1024:(par + 1) * 1024].rearrange(
            "p (dc h) -> p dc h", dc=DC)

        w12_rdy = {}
        w12_free = [[], []]
        w3_rdy = [0] * 8
        w3_free = [[] for _ in range(8)]
        glu_free = {}
        si_next = [0]

        def prefetch_w12():
            si = si_next[0]
            if si < len(wslices):
                e, hc = wslices[si]
                sA, sB = w12_src(e, hc)
                ld2(w12v(w1b, si % 2), sA, gpq=True, extra=w12_free[si % 2])
                w12_rdy[si] = ld2(w12v(w2b, si % 2), sB, gpq=True)
                w12_free[si % 2] = []
                si_next[0] += 1

        if "ffn" in stages:
            prefetch_w12()
            prefetch_w12()

        st_last = [0]
        st_par = [0, 0]
        res_free = [[], []]
        ln2_done = set()

        def emit_ln2_tile(i16, wv):
            ln2_done.add(i16)
            rpar = i16 % 2
            for dcq in range(2):
                bnm = ("U1a", "U1b")[dcq]
                bank = PSB2[bnm]
                for j in range(4):
                    dc = dcq * 4 + j
                    w = list(wv)
                    if j == 0:
                        w += bk2[bnm].acquire()
                        tguard2()
                    top2(lambda t, bank=bank, j=j, dc=dc, i16=i16:
                         t.transpose(bank[:, j * 128:(j + 1) * 128],
                                     x_fm3[:, dc, i16 * 128:(i16 + 1) * 128],
                                     ident[:]),
                         waits=w, inc=(1 if j == 3 else None))
                pe_tp = P2.n("pe")
                P2.op("v", lambda v, bank=bank, dcq=dcq, rpar=rpar:
                      v.tensor_copy(res_tm[:, rpar * 1024 + dcq * 512:
                                           rpar * 1024 + (dcq + 1) * 512],
                                    bank[:]),
                      waits=[("pe", pe_tp)] + res_free[rpar], inc=1)
                res_free[rpar] = []
                bk2[bnm].read("dv", P2.n("dv"))
            dv_res = P2.n("dv")
            R = res_tm[:, rpar * 1024:(rpar + 1) * 1024]
            P2.op("v", lambda v, R=R: v.reduce_sum(lnc[:, 0:1], R, axis=AX.X),
                  waits=[("dv", dv_res)], inc=1)
            P2.op("s", lambda s, R=R: s.activation(sqt[:], R[:, 0:512],
                                                   AF.Square, bias=0.0, scale=1.0),
                  waits=[("dv", dv_res), ("dv", P2.n("dv"))], inc=1)
            ac_q1 = P2.n("ac")
            P2.op("v", lambda v: v.reduce_sum(lnc[:, 1:2], sqt[:], axis=AX.X),
                  waits=[("ac", ac_q1)], inc=1)
            dv_q1 = P2.n("dv")
            P2.op("s", lambda s, R=R: s.activation(sqt[:], R[:, 512:1024],
                                                   AF.Square, bias=0.0, scale=1.0),
                  waits=[("dv", dv_q1)], inc=1)
            ac_q2 = P2.n("ac")
            P2.op("v", lambda v: v.reduce_sum(lnc[:, 3:4], sqt[:], axis=AX.X),
                  waits=[("ac", ac_q2)], inc=1)
            dv_s1 = P2.n("dv")
            ac_s2 = P2.n("ac")
            P2.op("v", lambda v: v.tensor_scalar_mul(lnc[:, 2:3], lnc[:, 0:1],
                                                     1.0 / D),
                  waits=[("dv", dv_s1), ("ac", ac_s2)], inc=1)
            dv_mm = P2.n("dv")
            P2.op("v", lambda v: v.tensor_tensor(lnc[:, 4:5], lnc[:, 1:2],
                                                 lnc[:, 3:4], ALU.add),
                  waits=[("dv", dv_mm)], inc=1)
            dv_sq = P2.n("dv")
            P2.op("v", lambda v: v.tensor_scalar_mul(lnc[:, 3:4], lnc[:, 4:5],
                                                     1.0 / D),
                  waits=[("dv", dv_sq)], inc=1)
            dv_m = P2.n("dv")
            P2.op("v", lambda v: v.tensor_tensor(lnc[:, 4:5], lnc[:, 2:3],
                                                 lnc[:, 2:3], ALU.mult),
                  waits=[("dv", dv_m)], inc=1)
            dv_m2 = P2.n("dv")
            P2.op("v", lambda v: v.tensor_tensor(lnc[:, 5:6], lnc[:, 3:4],
                                                 lnc[:, 4:5], ALU.subtract),
                  waits=[("dv", dv_m2)], inc=1)
            dv_m3 = P2.n("dv")
            P2.op("v", lambda v: v.tensor_scalar_add(lnc[:, 6:7], lnc[:, 5:6],
                                                     EPS), waits=[("dv", dv_m3)],
                  inc=1)
            dv_v = P2.n("dv")
            P2.op("s", lambda s: s.activation(lnc[:, 7:8], lnc[:, 6:7], AF.Sqrt,
                                              bias=0.0, scale=1.0),
                  waits=[("dv", dv_v)], inc=1)
            ac_sd = P2.n("ac")
            P2.op("v", lambda v: v.reciprocal(lnc[:, 5:6], lnc[:, 7:8]),
                  waits=[("ac", ac_sd)], inc=1)
            dv_r = P2.n("dv")
            O = out_tm[:, rpar * 1024:(rpar + 1) * 1024]
            P2.op("v", lambda v, R=R, O=O: v.tensor_scalar(O, R, lnc[:, 2:3],
                                                      lnc[:, 5:6], ALU.subtract,
                                                      ALU.mult),
                  waits=[("dv", dv_r), ("st", st_par[rpar])], inc=1)
            dv_o = P2.n("dv")
            res_free[rpar] = [("dv", dv_o)]
            P2.op("sync", lambda en, i16=i16, O=O:
                  en.dma_start(out=y[i16 * 128:(i16 + 1) * 128, :], in_=O),
                  waits=[("dv", dv_o)], inc=("st", 16))
            st_last[0] = P2.n("st")
            st_par[rpar] = P2.n("st")


        for ei in range(NE if "ffn" in stages else 0):
            e = EORDER[ei]
            for tt in range(TT):
                tguard2()
                top2(lambda t, e=e, tt=tt:
                     t.matmul(MBp[:], eye10[0:NE, e * 128:(e + 1) * 128],
                              m_fm[0:NE, tt * 512:(tt + 1) * 512],
                              start=True, stop=True),
                     waits=[("ld", n_eye),
                            ("dv", dv_mfm_init if e >= 8 else dv_mfm)]
                     + bk2["MB"].acquire(), inc=1)
                pe_mb = P2.n("pe")
                P2.op("v", lambda v, tt=tt:
                      v.tensor_copy(m_bc[:, tt * 512:(tt + 1) * 512], MBp[:]),
                      waits=[("pe", pe_mb)], inc=1)
                bk2["MB"].read("dv", P2.n("dv"))
            dv_mbc = P2.n("dv")
            tguard2()
            for hh in range(2):
                for hc8 in range(8):
                    hcg = hh * 8 + hc8
                    w3_rdy[hc8] = ld2(w3r3[:, hc8, :], w3_src(e, hcg), gpq=True,
                                      extra=w3_free[hc8])
                    w3_free[hc8] = []
                for hc8 in range(8):
                    hcg = hh * 8 + hc8
                    si = ei * 16 + hcg
                    par = si % 2
                    for tt in range(TT):
                        up = tt % 2
                        u1nm, u2nm = ("U1a", "U2a") if up == 0 else ("U1b", "U2b")
                        U1, U2 = PSB2[u1nm], PSB2[u2nm]
                        for dc in range(DC):
                            w = [("wg", w12_rdy[si])]
                            if dc == 0:
                                w += bk2[u1nm].acquire()
                            top2(lambda t, U1=U1, par=par, dc=dc, tt=tt:
                                 t.matmul(U1[:],
                                          w1b[:, par * 1024 + dc * 128:
                                              par * 1024 + (dc + 1) * 128],
                                          x_bf3[:, dc, tt * 512:(tt + 1) * 512],
                                          start=(dc == 0), stop=(dc == DC - 1)),
                                 waits=w, inc=(1 if dc == DC - 1 else None))
                        for dc in range(DC):
                            w = []
                            if dc == 0:
                                w += bk2[u2nm].acquire()
                            top2(lambda t, U2=U2, par=par, dc=dc, tt=tt:
                                 t.matmul(U2[:],
                                          w2b[:, par * 1024 + dc * 128:
                                              par * 1024 + (dc + 1) * 128],
                                          x_bf3[:, dc, tt * 512:(tt + 1) * 512],
                                          start=(dc == 0), stop=(dc == DC - 1)),
                                 waits=w, inc=(1 if dc == DC - 1 else None))
                        pe_up = P2.n("pe")
                        scr = (fsc1, fsc2)[up]
                        bcol = hcg * 10 + e
                        P2.op("s", lambda s, U1=U1, scr=scr, bcol=bcol:
                              s.activation(scr[:], U1[:], AF.Silu,
                                           bias=b1c[:, bcol:bcol + 1], scale=1.0),
                              waits=[("pe", pe_up), ("dv", P2.n("dv"))], inc=1)
                        ac_si = P2.n("ac")
                        bk2[u1nm].read("ac", ac_si)
                        gfw = [("ac", ac_si)] + glu_free.pop((hc8, tt), [])
                        P2.op("v", lambda v, U2=U2, scr=scr, hc8=hc8, tt=tt,
                              bcol=bcol:
                              v.scalar_tensor_tensor(
                                  glu3[:, hc8, tt * 512:(tt + 1) * 512], U2[:],
                                  b2c[:, bcol:bcol + 1], scr[:],
                                  ALU.add, ALU.mult),
                              waits=gfw, inc=1)
                        bk2[u2nm].read("dv", P2.n("dv"))
                        P2.op("v", lambda v, hc8=hc8, tt=tt:
                              v.tensor_tensor(glu3[:, hc8, tt * 512:(tt + 1) * 512],
                                              glu3[:, hc8, tt * 512:(tt + 1) * 512],
                                              m_bc[:, tt * 512:(tt + 1) * 512],
                                              ALU.mult),
                              waits=[("dv", dv_mbc)], inc=1)
                    w12_free[par] = [("pe", P2.n("pe"))]
                    prefetch_w12()
                dv_glu = P2.n("dv")
                gidx = 0
                for tt in range(TT):
                    for dc in range(DC):
                        dnm = ("Da", "Db")[gidx % 2]
                        Dk = PSB2[dnm]
                        with_bias = (ei == 2 and hh == 0)
                        if with_bias:
                            tguard2()
                            top2(lambda t, Dk=Dk, dc=dc, tt=tt:
                                 t.matmul(Dk[:], b3[0:NE, dc * 128:(dc + 1) * 128],
                                          m_fm[0:NE, tt * 512:(tt + 1) * 512],
                                          start=True, stop=False),
                                 waits=[("dv", dv_glu), ("wg", n_b3)]
                                 + bk2[dnm].acquire())
                            tguard2()
                        for hc8 in range(8):
                            w = [("dv", dv_glu), ("wg", w3_rdy[hc8])]
                            if hc8 == 0 and not with_bias:
                                w += bk2[dnm].acquire()
                            top2(lambda t, Dk=Dk, hc8=hc8, dc=dc, tt=tt,
                                 wb=with_bias:
                                 t.matmul(Dk[:],
                                          w3r3[:, hc8, dc * 128:(dc + 1) * 128],
                                          glu3[:, hc8, tt * 512:(tt + 1) * 512],
                                          start=(hc8 == 0 and not wb),
                                          stop=(hc8 == 7)),
                                 waits=w, inc=(1 if hc8 == 7 else None))
                        pe_dn = P2.n("pe")
                        for hc8 in range(8):
                            glu_free[(hc8, tt)] = [("pe", pe_dn)]
                        P2.op("v", lambda v, Dk=Dk, dc=dc, tt=tt:
                              v.tensor_tensor(x_fm3[:, dc, tt * 512:(tt + 1) * 512],
                                              x_fm3[:, dc, tt * 512:(tt + 1) * 512],
                                              Dk[:], ALU.add),
                              waits=[("pe", pe_dn)], inc=1)
                        bk2[dnm].read("dv", P2.n("dv"))
                        gidx += 1
                    if ei == NE - 1 and hh == 1:
                        dv_tt = P2.n("dv")
                        ac_tt = P2.n("ac")
                        for i16 in range(tt * 4, tt * 4 + 4):
                            emit_ln2_tile(i16, [("dv", dv_tt), ("ac", ac_tt)])
                pe_dn_all = P2.n("pe")
                for hc8 in range(8):
                    w3_free[hc8] = [("pe", pe_dn_all)]
        dv_ffn = P2.n("dv")
        ac_ffn = P2.n("ac")
        for i16 in range(T128):
            if i16 not in ln2_done:
                emit_ln2_tile(i16, [("dv", dv_ffn), ("ac", ac_ffn)])

        if "dbg" in stages:
            dbg1 = nc.dram_tensor("dbg_lnc", [128, 8], F32, kind="ExternalOutput").ap()
            dbg2 = nc.dram_tensor("dbg_sqt", [128, 1024], F32, kind="ExternalOutput").ap()
            dbg3 = nc.dram_tensor("dbg_res", [128, 2048], F32, kind="ExternalOutput").ap()
            for dst, srcb in ((dbg1, lnc[:]), (dbg2, sqt[:]), (dbg3, res_tm[:])):
                P2.op("sync", lambda e, d=dst, s=srcb: e.dma_start(out=d, in_=s),
                      waits=[("dv", P2.n("dv")), ("ac", P2.n("ac"))], inc=("st", 16))
            st_last[0] = P2.n("st")

        P2.op("v", lambda v: v.tensor_scalar_mul(lnc[:, 3:4], lnc[:, 2:3], 1.0),
              waits=[("st", st_last[0])], inc=1)

        with nc.Block() as blk2:
            P2.emit(blk2, sems)
        es2.close()

    return nc


_NC_CACHE = {}

def _get_nc():
    if "nc" not in _NC_CACHE:
        _NC_CACHE["nc"] = build_nc()
    return _NC_CACHE["nc"]


def kernel(**inputs):
    consts = _consts()
    x = np.ascontiguousarray(inputs["L0_attn_x"], dtype=np.float32)
    base = {}
    for k, v in inputs.items():
        if k == "L0_attn_x":
            continue
        nm = k.replace("L0_attn_", "").replace("L0_ffn_", "")
        base[nm] = np.ascontiguousarray(v, dtype=np.float32)
    base.update(consts)
    nc = _get_nc()
    in_maps = []
    for c in range(NCORE):
        m = dict(base)
        m["x"] = np.ascontiguousarray(x[c])
        in_maps.append(m)
    res = run_bass_kernel_spmd(nc, in_maps, core_ids=list(range(NCORE)))
    return np.stack([np.asarray(r["y"], dtype=np.float32) for r in res.results],
                    axis=0)


# version 11
# speedup vs baseline: 1.0780x; 1.0780x over previous
import numpy as np
import ml_dtypes
from contextlib import ExitStack

import concourse.bass as bass
import concourse.mybir as mybir
from concourse.bass_utils import run_bass_kernel_spmd

F32 = mybir.dt.float32
BF16 = mybir.dt.bfloat16
AF = mybir.ActivationFunctionType
ALU = mybir.AluOpType
AX = mybir.AxisListType

D = 1024
S = 2048
NH = 16
DH = 64
H = 2048
NE = 10
NCORE = 8
SCALE = 0.125
EPS = 1e-5
DC = D // 128
TT = S // 512
T128 = S // 128


def _consts():
    bfa = lambda a: np.ascontiguousarray(a).astype(ml_dtypes.bfloat16)
    c = {}
    c["ident"] = np.eye(128, dtype=np.float32)
    c["nident"] = -np.eye(128, dtype=np.float32)
    c["eyeb"] = bfa(np.eye(128, dtype=np.float32))
    inv = 1.0 / (10000.0 ** (np.arange(0, DH, 2, dtype=np.float32) / DH))
    t = np.arange(S, dtype=np.float32)
    fr = np.concatenate([np.outer(t, inv), np.outer(t, inv)], axis=1)
    cfm = np.cos(fr).T
    sfm = np.sin(fr).T
    c["cosb"] = bfa(np.concatenate([cfm, cfm], axis=0))
    c["sinb"] = bfa(np.concatenate([sfm, sfm], axis=0))
    R2 = np.zeros((128, 128), dtype=np.float32)
    for hb in (0, 64):
        for d in range(32):
            R2[hb + d, hb + d + 32] = -1.0
            R2[hb + d + 32, hb + d] = 1.0
    c["R2T"] = bfa(R2.T)
    p = np.arange(128)[:, None]
    cidx = np.arange(896)[None, :]
    c["maskq"] = bfa((cidx >= (p + 384)).astype(np.float32))
    c["onesbr"] = bfa(np.ones((1, 128), dtype=np.float32))
    c["ones32r"] = np.ones((1, 128), dtype=np.float32)
    c["ones32c"] = np.ones((128, 1), dtype=np.float32)
    ey = np.zeros((NE, NE * 128), dtype=np.float32)
    for e in range(NE):
        ey[e, e * 128:(e + 1) * 128] = 1.0
    c["eye10b"] = bfa(ey)
    return c


class Prog:
    ENG_SEM = {"t": "pe", "v": "dv", "s": "ac"}

    def __init__(self, cnt=None, wfloor=None):
        self.ops = {k: [] for k in ("t", "v", "s", "sync", "gp")}
        self.cnt = dict(cnt) if cnt else {k: 0 for k in
                                          ("pe", "dv", "ac", "ld", "st", "wg", "pl")}
        self.wmax = {k: (dict(wfloor) if wfloor else {}) for k in self.ops}

    def op(self, eng, fn, waits=(), inc=None):
        wl = []
        for snm, val in waits:
            if not val or val <= 0:
                continue
            if self.wmax[eng].get(snm, 0) >= val:
                continue
            self.wmax[eng][snm] = val
            wl.append((snm, val))
        icm = None
        if inc is not None:
            snm, k = inc if isinstance(inc, tuple) else (self.ENG_SEM[eng], inc)
            self.cnt[snm] += k
            icm = (snm, k)
        self.ops[eng].append((fn, wl, icm))
        return self.cnt[icm[0]] if icm else None

    def n(self, sem):
        return self.cnt[sem]

    def emit(self, block, sems):
        def mk(eng_key):
            def body(eng):
                for fn, wl, icm in self.ops[eng_key]:
                    for snm, val in wl:
                        eng.wait_ge(sems[snm], val)
                    ins = fn(eng)
                    if icm is not None:
                        ins.then_inc(sems[icm[0]], icm[1])
            return body
        for key, reg in (("sync", block.sync), ("gp", block.gpsimd),
                         ("t", block.tensor), ("v", block.vector),
                         ("s", block.scalar)):
            if self.ops[key]:
                reg(mk(key))


class Bank:
    def __init__(self):
        self.w = []

    def acquire(self):
        w, self.w = self.w, []
        return w

    def read(self, sem, cnt):
        self.w.append((sem, cnt))


def build_nc(stages=("msets", "msetv", "bias", "xin", "qkv", "rope", "vt", "attn", "proj", "ln1", "gate", "ffn")):
    stages = set(stages)
    nc = bass.Bass("TRN2", target_bir_lowering=False,
                   detect_race_conditions=False)

    ap = {}
    def din(name, shape, dt=F32):
        ap[name] = nc.dram_tensor(name, list(shape), dt, kind="ExternalInput").ap()
    din("x", (S, D))
    din("qkv_w", (3 * D, D)); din("qkv_b", (3 * D,))
    din("proj_w", (D, D)); din("proj_b", (D,))
    din("gate_w", (D, 8)); din("gate_b", (8,))
    din("rw1", (8, D, H)); din("rw2", (8, D, H)); din("rw3", (8, H, D))
    din("rb1", (8, H)); din("rb2", (8, H)); din("rb3", (8, D))
    din("sw1", (2, D, H)); din("sw2", (2, D, H)); din("sw3", (2, H, D))
    din("sb1", (2, H)); din("sb2", (2, H)); din("sb3", (2, D))
    for nm, shape, dt in [("ident", (128, 128), F32), ("nident", (128, 128), F32),
                          ("eyeb", (128, 128), BF16),
                          ("cosb", (128, S), BF16), ("sinb", (128, S), BF16),
                          ("R2T", (128, 128), BF16), ("maskq", (128, 896), BF16),
                          ("onesbr", (1, 128), BF16), ("ones32r", (1, 128), F32),
                          ("ones32c", (128, 1), F32),
                          ("eye10b", (NE, NE * 128), BF16)]:
        din(nm, shape, dt)
    y = nc.dram_tensor("y", [S, D], F32, kind="ExternalOutput").ap()

    es = ExitStack()
    sb = lambda nm, shape, dt: es.enter_context(nc.sbuf_tensor(nm, list(shape), dt))

    with es:
        x_fm = sb("x_fm", (128, DC * S), F32)
        x_bf = sb("x_bf", (128, DC * S), BF16)
        ident = sb("s_ident", (128, 128), F32)
        nident = sb("s_nident", (128, 128), F32)
        onesbr = sb("s_onesbr", (1, 128), BF16)
        ones32r = sb("s_ones32r", (1, 128), F32)
        ones32c = sb("s_ones32c", (128, 1), F32)
        qkvb = sb("qkvb", (128, 24), F32)
        projb = sb("projb", (128, 8), F32)
        gateb = sb("gateb", (8, 1), F32)
        b1c = sb("b1c", (128, 160), F32)
        b2c = sb("b2c", (128, 160), F32)
        b3 = sb("b3", (NE, D), BF16)
        gw = sb("gw", (128, 64), F32)
        m_fm = sb("m_fm", (NE, S), BF16)
        lnrow = sb("lnrow", (1, 2048), F32)

        x_fm3 = x_fm[:].rearrange("p (dc t) -> p dc t", dc=DC)
        x_bf3 = x_bf[:].rearrange("p (dc t) -> p dc t", dc=DC)

        sems = {s: es.enter_context(nc.semaphore(s))
                for s in ("ld", "st", "wg", "pe", "dv", "ac", "pl")}

        P = Prog()
        es1 = ExitStack()
        s1 = lambda nm, shape, dt: es1.enter_context(nc.sbuf_tensor(nm, list(shape), dt))
        es_b = ExitStack()
        b1s = es_b.enter_context(nc.sbuf_tensor("b1s", [NE, H], F32, side="right"))
        b2s = es_b.enter_context(nc.sbuf_tensor("b2s", [NE, H], F32, side="right"))
        qb24 = es_b.enter_context(nc.sbuf_tensor("qb24", [24, 128], F32, side="right"))
        pb8 = es_b.enter_context(nc.sbuf_tensor("pb8", [8, 128], F32, side="right"))
        cosb = s1("sb_cosb", (128, S), BF16)
        sinb = s1("sb_sinb", (128, S), BF16)
        R2T = s1("sb_R2T", (128, 128), BF16)
        eyeb = s1("sb_eyeb", (128, 128), BF16)
        maskq = s1("sb_maskq", (128, 896), BF16)
        stage = s1("stage", (128, 2048), F32)
        wqkvT = s1("wqkvT", (128, 2048), BF16)
        qraw = s1("qraw", (128, S), BF16)
        qst = s1("qst", (128, 2 * S), BF16)
        kst = s1("kst", (128, 2 * S), BF16)
        vT = s1("vT", (128, 16 * 132), BF16)
        probsT = s1("probsT", (128, 7 * 512), BF16)
        scr1 = s1("scr1", (128, 512), F32)
        scr2 = s1("scr2", (128, 512), F32)
        mxc = s1("mxc", (128, 72), F32)
        vscr = s1("vscr", (128, 512), F32)
        srow = lnrow[0:1, 0:512]
        rrow = lnrow[0:1, 512:1024]
        lrow = lnrow[0:1, 1024:1536]
        ac_lrow_free = [0]
        sumbc = s1("sumbc", (64, 512), F32)

        vT4 = vT[:].rearrange("p (kt h c) -> p kt h c", kt=16, h=2)
        probsT3 = probsT[:].rearrange("p (sl c) -> p sl c", sl=7)

        S0 = es1.enter_context(nc.psum_tensor("pS0", [128, 512], F32))
        S1b = es1.enter_context(nc.psum_tensor("pS1", [128, 512], F32))
        T0 = es1.enter_context(nc.psum_tensor("pT0", [128, 512], F32))
        T1 = es1.enter_context(nc.psum_tensor("pT1", [128, 512], F32))
        C0 = es1.enter_context(nc.psum_tensor("pC0", [128, 512], F32))
        C1 = es1.enter_context(nc.psum_tensor("pC1", [128, 512], F32))
        RB = es1.enter_context(nc.psum_tensor("pRB", [128, 512], F32))
        vtp = es1.enter_context(nc.psum_tensor("vtp", [128, 512], F32))

        bk = {t_: Bank() for t_ in ("S0", "S1", "T0", "T1", "C0", "C1", "RB", "VT")}
        PSB = {"S0": S0, "S1": S1b, "T0": T0, "T1": T1, "C0": C0, "C1": C1, "RB": RB}

        pend = [0]
        def tguard():
            pend[0] = P.n("pe")

        def top(fn, waits=(), inc=None):
            w = list(waits)
            if pend[0]:
                w.append(("pe", pend[0]))
                pend[0] = 0
            return P.op("t", fn, waits=w, inc=inc)

        def ld(dst_ap, src_ap, gpq=False, extra=()):
            eng, semn = ("gp", "wg") if gpq else ("sync", "ld")
            P.op(eng, lambda e, d=dst_ap, s=src_ap: e.dma_start(out=d, in_=s),
                 waits=extra, inc=(semn, 16))
            return P.n(semn)

        for dst, src in [(ident[:], ap["ident"][:]), (nident[:], ap["nident"][:]),
                         (onesbr[:], ap["onesbr"][:]),
                         (ones32r[:], ap["ones32r"][:]), (ones32c[:], ap["ones32c"][:]),
                         (cosb[:], ap["cosb"][:]), (sinb[:], ap["sinb"][:]),
                         (R2T[:], ap["R2T"][:]), (eyeb[:], ap["eyeb"][:]),
                         (maskq[:], ap["maskq"][:]),
                         (qb24[:], ap["qkv_b"][:].rearrange("(c p) -> c p", c=24)),
                         (pb8[:], ap["proj_b"][:].rearrange("(c p) -> c p", c=8)),
                         (gateb[:], ap["gate_b"][:].unsqueeze(1)),
                         (b1s[0:8, :], ap["rb1"][:]), (b1s[8:10, :], ap["sb1"][:]),
                         (b2s[0:8, :], ap["rb2"][:]), (b2s[8:10, :], ap["sb2"][:])]:
            ld(dst, src)
        ld(gw[:].rearrange("p (dc e) -> p dc e", dc=DC),
           ap["gate_w"][:].rearrange("(dc p) e -> p dc e", p=128))
        n_ld0 = P.n("ld")

        dv_mfm_init = dv_vT_ones = 0
        if "msets" in stages:
            P.op("v", lambda v: v.memset(m_fm[:], 0.5), inc=1)
            dv_mfm_init = P.n("dv")
        if "msetv" in stages:
            P.op("v", lambda v: v.memset(vT[:], 1.0), inc=1)
            P.op("v", lambda v: v.memset(kst[:], 1.0), inc=1)
            dv_vT_ones = P.n("dv")

        wld = [("ld", n_ld0)]
        if "bias" not in stages:
            es_b.close()
            ctx = s1("ctx", (128, DC * S), BF16)
            ctx3 = ctx[:].rearrange("p (dc t) -> p dc t", dc=DC)
        if "bias" in stages:
          tguard()
          top(lambda t: t.transpose(S0[:, 0:24], qb24[:], ident[0:24, 0:24]),
            waits=wld + bk["S0"].acquire(), inc=1)
          tguard()
          top(lambda t: t.transpose(S0[:, 24:32], pb8[:], ident[0:8, 0:8]), inc=1)
          pe_bt = P.n("pe")
          P.op("v", lambda v: v.tensor_copy(qkvb[:], S0[:, 0:24]),
             waits=[("pe", pe_bt)], inc=1)
          P.op("v", lambda v: v.tensor_copy(projb[:], S0[:, 24:32]), inc=1)
          bk["S0"].read("dv", P.n("dv"))
          for hc in range(16):
            tguard()
            top(lambda t, hc=hc: t.transpose(S1b[:, hc * 10:hc * 10 + 10],
                                             b1s[0:NE, hc * 128:(hc + 1) * 128],
                                             ident[0:NE, 0:NE]),
                waits=(wld + bk["S1"].acquire()) if hc == 0 else wld, inc=1)
            tguard()
            top(lambda t, hc=hc: t.transpose(T0[:, hc * 10:hc * 10 + 10],
                                             b2s[0:NE, hc * 128:(hc + 1) * 128],
                                             ident[0:NE, 0:NE]),
                waits=bk["T0"].acquire() if hc == 0 else (), inc=1)
          pe_b12 = P.n("pe")
          P.op("v", lambda v: v.tensor_copy(b1c[:], S1b[:, 0:160]),
             waits=[("pe", pe_b12)], inc=1)
          P.op("v", lambda v: v.tensor_copy(b2c[:], T0[:, 0:160]), inc=1)
          bk["S1"].read("dv", P.n("dv"))
          bk["T0"].read("dv", P.n("dv"))
          tguard()
          es_b.close()
          ctx = s1("ctx", (128, DC * S), BF16)
          ctx3 = ctx[:].rearrange("p (dc t) -> p dc t", dc=DC)

        stage_free = [[], []]

        def load_stage(par, src_ap):
            nld = ld(stage[:, par * 1024:(par + 1) * 1024], src_ap,
                     extra=stage_free[par])
            stage_free[par] = []
            return nld

        def transpose_chunk(par, nld, banks, evac):
            for dcq in range(2):
                bnm = banks[dcq]
                bank = PSB[bnm]
                for j in range(4):
                    dc = dcq * 4 + j
                    w = [("ld", nld)]
                    if j == 0:
                        w += bk[bnm].acquire()
                    top(lambda t, bank=bank, j=j, par=par, dc=dc:
                        t.transpose(bank[:, j * 128:(j + 1) * 128],
                                    stage[:, par * 1024 + dc * 128:
                                          par * 1024 + (dc + 1) * 128],
                                    ident[:]),
                        waits=w, inc=(1 if dc == 7 else None))
            stage_free[par] = [("pe", P.n("pe"))]
            pe_tp = P.n("pe")
            for dcq in range(2):
                evac(banks[dcq], dcq, pe_tp)

        for i16 in range(T128 if "xin" in stages else 0):
            par = i16 % 2
            nld = load_stage(par, ap["x"][i16 * 128:(i16 + 1) * 128, :])
            banks = ("T0", "T1") if i16 % 2 == 0 else ("C0", "C1")

            def evac_x(bnm, dcq, pe_tp, i16=i16):
                bank = PSB[bnm]
                src = bank[:].rearrange("p (j c) -> p j c", j=4)
                P.op("v", lambda v, src=src, dcq=dcq, i16=i16:
                     v.tensor_copy(x_fm3[:, dcq * 4:(dcq + 1) * 4,
                                         i16 * 128:(i16 + 1) * 128], src),
                     waits=[("pe", pe_tp)], inc=1)
                dv_cp = P.n("dv")
                bk[bnm].read("dv", dv_cp)
                P.op("s", lambda s, dcq=dcq, i16=i16:
                     s.copy(x_bf3[:, dcq * 4:(dcq + 1) * 4,
                                  i16 * 128:(i16 + 1) * 128],
                            x_fm3[:, dcq * 4:(dcq + 1) * 4,
                                  i16 * 128:(i16 + 1) * 128]),
                     waits=[("dv", dv_cp)], inc=1)
            transpose_chunk(par, nld, banks, evac_x)

        wdst_free = {"wq": [[], []]}

        def stream_wchunk(src_rows, dstbuf, par, banks, key):
            nld = load_stage(par, src_rows)

            def evac_w(bnm, dcq, pe_tp):
                bank = PSB[bnm]
                P.op("s", lambda s, bank=bank, dcq=dcq:
                     s.copy(dstbuf[:, par * 1024 + dcq * 512:
                                   par * 1024 + (dcq + 1) * 512], bank[:]),
                     waits=[("pe", pe_tp)] + wdst_free[key][par], inc=1)
                bk[bnm].read("ac", P.n("ac"))
            transpose_chunk(par, nld, banks, evac_w)
            wdst_free[key][par] = []
            return P.n("ac")

        vtp_slot_free = [[] for _ in range(4)]
        raw_free = [[]]
        scr2_free = [[]]
        buf_free = [[]]
        chunk_i = [0]
        unit_i = [0]
        qk_ready = [0]

        def qkv_chunk(c, kind):
            par = chunk_i[0] % 2
            banks = ("T0", "T1") if chunk_i[0] % 2 == 0 else ("C0", "C1")
            wrdy = stream_wchunk(ap["qkv_w"][c * 128:(c + 1) * 128, :],
                                 wqkvT, par, banks, "wq")
            mmbanks = ("S0", "S1")
            for tt in range(TT):
                bnm = mmbanks[tt % 2]
                bank = PSB[bnm]
                for dc in range(DC):
                    w = [("ac", wrdy)]
                    if dc == 0:
                        w += bk[bnm].acquire()
                    top(lambda t, bank=bank, par=par, dc=dc, tt=tt:
                        t.matmul(bank[:],
                                 wqkvT[:, par * 1024 + dc * 128:
                                       par * 1024 + (dc + 1) * 128],
                                 x_bf3[:, dc, tt * 512:(tt + 1) * 512],
                                 start=(dc == 0), stop=(dc == DC - 1)),
                        waits=w, inc=(1 if dc == DC - 1 else None))
                pe_mm = P.n("pe")
                evac_dst = (qraw[:, tt * 512:(tt + 1) * 512] if kind != "v"
                            else vscr[:])
                P.op("s", lambda s, bank=bank, c=c, d=evac_dst:
                     s.activation(d, bank[:], AF.Identity,
                                  bias=qkvb[:, c:c + 1], scale=1.0),
                     waits=[("pe", pe_mm)] + raw_free[0], inc=1)
                raw_free[0] = []
                ac_evac = P.n("ac")
                bk[bnm].read("ac", ac_evac)
                if kind in ("q", "k") and "rope" in stages:
                    dstbuf = qst if kind == "q" else kst
                    tguard()
                    top(lambda t, tt=tt: t.matmul(RB[:], R2T[:],
                                                  qraw[:, tt * 512:(tt + 1) * 512],
                                                  start=True, stop=True),
                        waits=[("ac", ac_evac)] + bk["RB"].acquire(), inc=1)
                    pe_rot = P.n("pe")
                    P.op("v", lambda v, tt=tt: v.tensor_tensor(
                        scr1[:], RB[:], sinb[:, tt * 512:(tt + 1) * 512], ALU.mult),
                        waits=[("pe", pe_rot)], inc=1)
                    bk["RB"].read("dv", P.n("dv"))
                    P.op("v", lambda v, tt=tt: v.tensor_tensor(
                        scr2[:], qraw[:, tt * 512:(tt + 1) * 512],
                        cosb[:, tt * 512:(tt + 1) * 512], ALU.mult),
                        waits=scr2_free[0], inc=1)
                    wg_cos = ("dv", P.n("dv"))
                    for h2 in range(2):
                        P.op("v", lambda v, tt=tt, dstbuf=dstbuf, h2=h2:
                             v.tensor_tensor(
                                 dstbuf[0:64, h2 * S + tt * 512:
                                        h2 * S + (tt + 1) * 512],
                                 scr1[h2 * 64:h2 * 64 + 64, :],
                                 scr2[h2 * 64:h2 * 64 + 64, :], ALU.add),
                             waits=buf_free[0] + [wg_cos], inc=1)
                        buf_free[0] = []
                    scr2_free[0] = [("dv", P.n("dv"))]
                    raw_free[0] = [("dv", P.n("dv")), ("pe", pe_rot)]
                elif kind == "v" and "vt" in stages:
                    for j in range(4):
                        w = [("ac", ac_evac)] + vtp_slot_free[0]
                        vtp_slot_free[0] = []
                        top(lambda t, j=j:
                            t.transpose(vtp[:, j * 128:(j + 1) * 128],
                                        vscr[:, j * 128:(j + 1) * 128],
                                        ident[:]),
                            waits=w, inc=(1 if j == 3 else None))
                    pe_v = P.n("pe")
                    for j in range(4):
                        i16 = tt * 4 + j
                        for hh2 in range(2):
                            P.op("s", lambda s, i16=i16, j=j, hh2=hh2:
                                 s.copy(
                                     vT4[:, i16, hh2, 0:64],
                                     vtp[:, j * 128 + hh2 * 64:
                                         j * 128 + hh2 * 64 + 64]),
                                 waits=[("pe", pe_v), ("dv", dv_vT_ones)]
                                 + buf_free[0], inc=(1 if (j == 3 and hh2 == 1)
                                                     else None))
                            buf_free[0] = []
                    vtp_slot_free[0] = [("ac", P.n("ac"))]
                    raw_free[0] = [("pe", P.n("pe")), ("ac", P.n("ac"))]
            chunk_i[0] += 1

        slot_free = [0] * 7
        stA = {}

        aidx = [0]

        def stageA(h, si):
            h2 = h % 2
            qc0 = si * 512
            mpar = 24 * (aidx[0] % 3)
            aidx[0] += 1
            for qt in range(4):
                qtg = 4 * si + qt
                ktot = (qtg + 1) * 128
                nks = (ktot + 511) // 512
                tguard()
                for ks in range(nks):
                    kw = min(512, ktot - ks * 512)
                    bnm = ("S0", "S1")[ks % 2]
                    bank = PSB[bnm]
                    w = bk[bnm].acquire() + [("dv", qk_ready[0])]
                    top(lambda t, bank=bank, kw=kw, ks=ks, qtg=qtg, h2=h2:
                        t.matmul(bank[:, 0:kw],
                                 qst[0:64, h2 * S + qtg * 128:
                                     h2 * S + (qtg + 1) * 128],
                                 kst[0:64, h2 * S + ks * 512:
                                     h2 * S + ks * 512 + kw],
                                 start=True, stop=True),
                        waits=w, inc=1)
                    pe_sc = P.n("pe")
                    P.op("v", lambda v, bank=bank, kw=kw, col=mpar + qt * 4 + ks:
                         v.reduce_max(mxc[:, col:col + 1], bank[:, 0:kw],
                                      axis=AX.X),
                         waits=[("pe", pe_sc)], inc=1)
                    bk[bnm].read("dv", P.n("dv"))
                dvr = P.n("dv")
                if nks > 1:
                    P.op("v", lambda v, qt=qt, nks=nks, mp=mpar:
                         v.reduce_max(mxc[:, mp + 16 + qt:mp + 17 + qt],
                                      mxc[:, mp + qt * 4:mp + qt * 4 + nks],
                                      axis=AX.X),
                         waits=[("dv", dvr)], inc=1)
                    dvr2 = P.n("dv")
                    P.op("v", lambda v, qt=qt, mp=mpar: v.tensor_scalar_mul(
                        mxc[:, mp + 20 + qt:mp + 21 + qt],
                        mxc[:, mp + 16 + qt:mp + 17 + qt], -1.0),
                        waits=[("dv", dvr2)], inc=1)
                else:
                    P.op("v", lambda v, qt=qt, mp=mpar: v.tensor_scalar_mul(
                        mxc[:, mp + 20 + qt:mp + 21 + qt],
                        mxc[:, mp + qt * 4:mp + qt * 4 + 1], -1.0),
                        waits=[("dv", dvr)], inc=1)
            dv_mx = P.n("dv")
            tguard()
            for qt in range(4):
                w = [("dv", dv_mx)]
                if qt == 0:
                    w += bk["S0"].acquire()
                top(lambda t, qt=qt, mp=mpar:
                    t.transpose(S0[0:1, qt * 128:(qt + 1) * 128],
                                mxc[:, mp + 20 + qt:mp + 21 + qt],
                                ident[:]),
                    waits=w, inc=(1 if qt == 3 else None))
            pe_m1 = P.n("pe")
            P.op("v", lambda v, h2=h2, qc0=qc0:
                 v.tensor_copy(qst[64:65, h2 * S + qc0:h2 * S + qc0 + 512],
                               S0[0:1, 0:512]),
                 waits=[("pe", pe_m1)], inc=1)
            dv_m1 = P.n("dv")
            bk["S0"].read("dv", dv_m1)
            stA[(h, si)] = dv_m1

        def stageCD(h, si):
            h2 = h % 2
            hp = h // 2
            ktmax = 4 * si + 4
            u = unit_i[0]
            unit_i[0] += 1
            qc0 = si * 512
            dv_m1 = stA[(h, si)]
            cnm = ("C0", "C1")[u % 2]
            Cb = PSB[cnm]
            for k0 in range(0, ktmax, 7):
                k1 = min(k0 + 7, ktmax)
                ac_exp = {}
                dv_msk = {}
                for kt in range(k0, k1):
                    bnm = ("T0", "T1")[kt % 2]
                    bank = PSB[bnm]
                    slot = kt % 7
                    tguard()
                    w = [("dv", dv_m1), ("pe", slot_free[slot])]                         + bk[bnm].acquire()
                    top(lambda t, bank=bank, kt=kt, h2=h2, qc0=qc0:
                        t.matmul(bank[:],
                                 kst[0:65, h2 * S + kt * 128:
                                     h2 * S + (kt + 1) * 128],
                                 qst[0:65, h2 * S + qc0:h2 * S + qc0 + 512],
                                 start=True, stop=True),
                        waits=w, inc=1)
                    pe_st = P.n("pe")
                    P.op("s", lambda s, bank=bank, slot=slot:
                         s.activation(probsT3[:, slot, :], bank[:], AF.Exp,
                                      bias=0.0, scale=SCALE),
                         waits=[("pe", pe_st)], inc=1)
                    bk[bnm].read("ac", P.n("ac"))
                    ac_exp[kt] = P.n("ac")
                    if kt >= 4 * si:
                        o = kt - 4 * si
                        P.op("v", lambda v, slot=slot, o=o:
                             v.tensor_tensor(probsT3[:, slot, :],
                                             probsT3[:, slot, :],
                                             maskq[:, (3 - o) * 128:
                                                   (3 - o) * 128 + 512],
                                             ALU.mult),
                             waits=[("ac", ac_exp[kt])], inc=1)
                        dv_msk[kt] = ("dv", P.n("dv"))
                tguard()
                for kt in range(k0, k1):
                    w = [("ac", ac_exp[kt])]
                    if kt in dv_msk:
                        w.append(dv_msk[kt])
                    if kt == 0:
                        w += bk[cnm].acquire()
                    top(lambda t, Cb=Cb, kt=kt, h=h:
                        t.matmul(Cb[0:65, :], vT4[:, kt, h % 2, 0:65],
                                 probsT3[:, kt % 7, :],
                                 start=(kt == 0), stop=(kt == ktmax - 1)),
                        waits=w, inc=(1 if kt == k1 - 1 else None))
                pe_ck = P.n("pe")
                for kt in range(k0, k1):
                    slot_free[kt % 7] = pe_ck
            pe_ctx = P.n("pe")
            P.op("v", lambda v, Cb=Cb: v.tensor_copy(srow, Cb[64:65, 0:512]),
                 waits=[("pe", pe_ctx)], inc=1)
            dv_srow = P.n("dv")
            P.op("s", lambda s: s.activation(lrow, srow, AF.Ln,
                                             bias=0.0, scale=1.0),
                 waits=[("dv", dv_srow), ("ac", ac_lrow_free[0])], inc=1)
            ac_ln = P.n("ac")
            P.op("s", lambda s: s.activation(rrow, lrow, AF.Exp,
                                             bias=0.0, scale=-1.0),
                 waits=[("ac", ac_ln)], inc=1)
            dv_rr = ("ac", P.n("ac"))
            ac_lrow_free[0] = P.n("ac")
            tguard()
            top(lambda t: t.matmul(RB[0:64, :], ones32r[0:1, 0:64], rrow,
                                   start=True, stop=True),
                waits=[dv_rr] + bk["RB"].acquire(), inc=1)
            pe_rbc = P.n("pe")
            P.op("s", lambda s: s.copy(sumbc[:], RB[0:64, :]),
                 waits=[("pe", pe_rbc)], inc=1)
            bk["RB"].read("ac", P.n("ac"))
            dv_sbc = ("ac", P.n("ac"))
            P.op("v", lambda v, Cb=Cb, hp=hp, h=h, qc0=qc0:
                 v.tensor_tensor(ctx3[(h % 2) * 64:(h % 2) * 64 + 64, hp,
                                      qc0:qc0 + 512],
                                 Cb[0:64, :], sumbc[:], ALU.mult),
                 waits=[dv_sbc], inc=1)
            bk[cnm].read("dv", P.n("dv"))

        for hp in range(8 if "qkv" in stages else 0):
            qkv_chunk(hp, "q")
            qkv_chunk(8 + hp, "k")
            qkv_chunk(16 + hp, "v")
            qk_ready[0] = P.n("dv")
            if "attn" in stages:
                units = [(h, si) for h in (2 * hp, 2 * hp + 1)
                         for si in range(4)]
                stageA(*units[0])
                stageA(*units[1])
                stageA(*units[2])
                for i in range(len(units)):
                    if i + 3 < len(units):
                        stageA(*units[i + 3])
                    stageCD(*units[i])
            buf_free[0] = [("pe", P.n("pe"))]

        dv_attn = P.n("dv")
        ac_attn = P.n("ac")

        for dco in range(DC if "proj" in stages else 0):
            par = dco % 2
            banks = ("T0", "T1") if dco % 2 == 0 else ("C0", "C1")
            wrdy = stream_wchunk(ap["proj_w"][dco * 128:(dco + 1) * 128, :],
                                 wqkvT, par, banks, "wq")
            for tt in range(TT):
                bnm = ("S0", "S1")[tt % 2]
                bank = PSB[bnm]
                for dci in range(DC):
                    w = [("ac", wrdy), ("dv", dv_attn)]
                    if dci == 0:
                        w += bk[bnm].acquire()
                    top(lambda t, bank=bank, par=par, dci=dci, tt=tt:
                        t.matmul(bank[:],
                                 wqkvT[:, par * 1024 + dci * 128:
                                       par * 1024 + (dci + 1) * 128],
                                 ctx3[:, dci, tt * 512:(tt + 1) * 512],
                                 start=(dci == 0), stop=(dci == DC - 1)),
                        waits=w, inc=(1 if dci == DC - 1 else None))
                pe_p = P.n("pe")
                P.op("v", lambda v, bank=bank, dco=dco, tt=tt:
                     v.scalar_tensor_tensor(x_fm3[:, dco, tt * 512:(tt + 1) * 512],
                                            bank[:], projb[:, dco:dco + 1],
                                            x_fm3[:, dco, tt * 512:(tt + 1) * 512],
                                            ALU.add, ALU.add),
                     waits=[("pe", pe_p)], inc=1)
                bk[bnm].read("dv", P.n("dv"))
            wdst_free["wq"][par] = [("pe", P.n("pe"))]
        dv_proj = P.n("dv")

        scr_pe_free = [0, 0]
        for tt in range(TT if "ln1" in stages else 0):
            for dc in range(DC):
                w = [("dv", dv_proj)]
                if dc == 0:
                    w += bk["S0"].acquire()
                    tguard()
                top(lambda t, dc=dc, tt=tt:
                    t.matmul(S0[0:1, :], ones32c[:],
                             x_fm3[:, dc, tt * 512:(tt + 1) * 512],
                             start=(dc == 0), stop=(dc == DC - 1)),
                    waits=w, inc=(1 if dc == DC - 1 else None))
            for dc in range(DC):
                scr = (scr1, scr2)[dc % 2]
                P.op("s", lambda s, dc=dc, tt=tt, scr=scr:
                     s.activation(scr[:], x_fm3[:, dc, tt * 512:(tt + 1) * 512],
                                  AF.Square, bias=0.0, scale=1.0),
                     waits=[("dv", P.n("dv")), ("pe", scr_pe_free[dc % 2])], inc=1)
                ac_sq = P.n("ac")
                w = [("ac", ac_sq)]
                if dc == 0:
                    w += bk["S1"].acquire()
                top(lambda t, dc=dc, scr=scr:
                    t.matmul(S1b[0:1, :], ones32c[:], scr[:],
                             start=(dc == 0), stop=(dc == DC - 1)),
                    waits=w, inc=1)
                scr_pe_free[dc % 2] = P.n("pe")
            pe_sums = P.n("pe")
            L = lnrow
            P.op("v", lambda v: v.tensor_scalar_mul(L[0:1, 0:512], S0[0:1, :],
                                                    1.0 / D),
                 waits=[("pe", pe_sums)], inc=1)
            P.op("v", lambda v: v.tensor_scalar_mul(L[0:1, 1536:2048], S1b[0:1, :],
                                                    1.0 / D), inc=1)
            bk["S0"].read("dv", P.n("dv"))
            bk["S1"].read("dv", P.n("dv"))
            dvc = P.n("dv")
            P.op("v", lambda v: v.tensor_tensor(L[0:1, 1024:1536], L[0:1, 0:512],
                                                L[0:1, 0:512], ALU.mult),
                 waits=[("dv", dvc)], inc=1)
            P.op("v", lambda v: v.tensor_tensor(L[0:1, 1024:1536],
                                                L[0:1, 1536:2048],
                                                L[0:1, 1024:1536], ALU.subtract),
                 inc=1)
            P.op("v", lambda v: v.tensor_scalar_add(L[0:1, 1024:1536],
                                                    L[0:1, 1024:1536], EPS), inc=1)
            dvv = P.n("dv")
            P.op("s", lambda s: s.activation(L[0:1, 1536:2048], L[0:1, 1024:1536],
                                             AF.Sqrt, bias=0.0, scale=1.0),
                 waits=[("dv", dvv)], inc=1)
            acs = P.n("ac")
            P.op("v", lambda v: v.reciprocal(L[0:1, 512:1024], L[0:1, 1536:2048]),
                 waits=[("ac", acs)], inc=1)
            dvr = P.n("dv")
            P.op("v", lambda v: v.tensor_tensor(L[0:1, 1024:1536], L[0:1, 0:512],
                                                L[0:1, 512:1024], ALU.mult),
                 waits=[("dv", dvr)], inc=1)
            dvb0 = P.n("dv")
            P.op("v", lambda v: v.tensor_scalar_mul(L[0:1, 1024:1536],
                                                    L[0:1, 1024:1536], -1.0),
                 waits=[("dv", dvb0)], inc=1)
            dvab = P.n("dv")
            tguard()
            top(lambda t: t.matmul(T0[:], ones32r[:], L[0:1, 512:1024],
                                   start=True, stop=True),
                waits=[("dv", dvab)] + bk["T0"].acquire(), inc=1)
            tguard()
            top(lambda t: t.matmul(T1[:], ones32r[:], L[0:1, 1024:1536],
                                   start=True, stop=True),
                waits=bk["T1"].acquire(), inc=1)
            pe_bc = P.n("pe")
            for dc in range(DC):
                scr = (scr1, scr2)[dc % 2]
                P.op("v", lambda v, dc=dc, tt=tt, scr=scr:
                     v.tensor_tensor(scr[:], x_fm3[:, dc, tt * 512:(tt + 1) * 512],
                                     T0[:], ALU.mult),
                     waits=[("pe", pe_bc), ("ac", P.n("ac"))], inc=1)
                P.op("v", lambda v, dc=dc, tt=tt, scr=scr:
                     v.tensor_tensor(x_fm3[:, dc, tt * 512:(tt + 1) * 512],
                                     scr[:], T1[:], ALU.add), inc=1)
                dvxn = P.n("dv")
                P.op("s", lambda s, dc=dc, tt=tt:
                     s.copy(x_bf3[:, dc, tt * 512:(tt + 1) * 512],
                            x_fm3[:, dc, tt * 512:(tt + 1) * 512]),
                     waits=[("dv", dvxn)], inc=1)
            bk["T0"].read("dv", P.n("dv"))
            bk["T1"].read("dv", P.n("dv"))
        dv_ln1 = P.n("dv")
        ac_ln1 = P.n("ac")

        dv_mfm = 0

        with nc.Block() as blk1:
            P.emit(blk1, sems)
        es1.close()

        P2 = Prog(cnt=P.cnt, wfloor=P.cnt)
        pend2 = [0]

        def tguard2():
            pend2[0] = P2.n("pe")

        def top2(fn, waits=(), inc=None):
            w = list(waits)
            if pend2[0]:
                w.append(("pe", pend2[0]))
                pend2[0] = 0
            return P2.op("t", fn, waits=w, inc=inc)

        def ld2(dst_ap, src_ap, gpq=False, extra=()):
            eng, semn = ("gp", "wg") if gpq else ("sync", "ld")
            P2.op(eng, lambda e, d=dst_ap, s=src_ap: e.dma_start(out=d, in_=s),
                  waits=extra, inc=(semn, 16))
            return P2.n(semn)

        es2 = ExitStack()
        s2 = lambda nm, shape, dt: es2.enter_context(
            nc.sbuf_tensor(nm, list(shape), dt))
        glu = s2("glu", (128, 8 * S), BF16)
        w1b = s2("w1b", (128, 2048), BF16)
        w2b = s2("w2b", (128, 2048), BF16)
        w3r = s2("w3r", (128, 8 * 1024), BF16)
        m_bc = s2("m_bc", (128, S), BF16)
        eye10 = s2("eye10", (NE, NE * 128), BF16)
        fsc1 = s2("fsc1", (128, 512), F32)
        fsc2 = s2("fsc2", (128, 512), F32)
        res_tm = s2("res_tm", (128, 2048), F32)
        out_tm = s2("out_tm", (128, 2048), F32)
        sqt = s2("sqt", (128, 512), F32)
        lnc = s2("lnc", (128, 8), F32)
        g_s = s2("g_s", (8, S), F32)
        gtm = s2("gtm", (128, 8), F32)
        gsc = s2("gsc", (128, 48), F32)
        glu3 = glu[:].rearrange("p (hc t) -> p hc t", hc=8)
        w3r3 = w3r[:].rearrange("p (hc d) -> p hc d", hc=8)

        pU = [es2.enter_context(nc.psum_tensor(f"pu{i}", [128, 512], F32))
              for i in range(8)]
        U1a, U1b, U2a, U2b, Da, Db, MBp, LBp = pU
        bk2 = {t_: Bank() for t_ in ("U1a", "U1b", "U2a", "U2b", "Da", "Db",
                                     "MB", "LB")}
        PSB2 = {"U1a": U1a, "U1b": U1b, "U2a": U2a, "U2b": U2b,
                "Da": Da, "Db": Db, "MB": MBp, "LB": LBp}

        n_eye = ld2(eye10[:], ap["eye10b"][:])
        ld2(b3[0:8, :], ap["rb3"][:], gpq=True)
        n_b3 = ld2(b3[8:10, :], ap["sb3"][:], gpq=True)

        for tt in range(TT if "gate" in stages else 0):
            for dc in range(DC):
                w = [("ac", ac_ln1), ("dv", dv_ln1)]
                if dc == 0:
                    w += bk2["Da"].acquire()
                top2(lambda t, dc=dc, tt=tt:
                    t.matmul(Da[0:8, :], gw[:, dc * 8:(dc + 1) * 8],
                             x_fm3[:, dc, tt * 512:(tt + 1) * 512],
                             start=(dc == 0), stop=(dc == DC - 1)),
                    waits=w, inc=(1 if dc == DC - 1 else None))
            pe_g = P2.n("pe")
            P2.op("s", lambda s, tt=tt:
                 s.activation(g_s[0:8, tt * 512:(tt + 1) * 512], Da[0:8, :],
                              AF.Identity, bias=gateb[:, 0:1], scale=1.0),
                 waits=[("pe", pe_g)], inc=1)
            bk2["Da"].read("ac", P2.n("ac"))
        ac_gs = P2.n("ac")
        for i16 in range(T128 if "gate" in stages else 0):
            tguard2()
            top2(lambda t, i16=i16:
                t.transpose(Db[:, 0:8], g_s[0:8, i16 * 128:(i16 + 1) * 128],
                            ident[0:8, 0:8]),
                waits=[("ac", ac_gs)] + bk2["Db"].acquire(), inc=1)
            pe_gt = P2.n("pe")
            G = gsc
            P2.op("v", lambda v: v.tensor_copy(gtm[:], Db[:, 0:8]),
                 waits=[("pe", pe_gt)], inc=1)
            bk2["Db"].read("dv", P2.n("dv"))
            dv0 = P2.n("dv")
            P2.op("v", lambda v: v.reduce_max(G[:, 8:9], gtm[:], axis=AX.X),
                 waits=[("dv", dv0)], inc=1)
            dvA = P2.n("dv")
            P2.op("v", lambda v: v.tensor_scalar_mul(G[:, 9:10], G[:, 8:9], -1.0),
                 waits=[("dv", dvA)], inc=1)
            dvB = P2.n("dv")
            P2.op("s", lambda s: s.activation(G[:, 0:8], gtm[:], AF.Exp,
                                             bias=G[:, 9:10], scale=1.0),
                 waits=[("dv", dvB)], inc=1)
            acE = P2.n("ac")
            P2.op("v", lambda v: v.reduce_sum(G[:, 10:11], G[:, 0:8], axis=AX.X),
                 waits=[("ac", acE)], inc=1)
            dvS = P2.n("dv")
            P2.op("v", lambda v: v.reciprocal(G[:, 11:12], G[:, 10:11]),
                 waits=[("dv", dvS)], inc=1)
            dvR = P2.n("dv")
            P2.op("v", lambda v: v.tensor_scalar_mul(G[:, 16:24], G[:, 0:8],
                                                    G[:, 11:12]),
                 waits=[("dv", dvR)], inc=1)
            dvP = P2.n("dv")
            P2.op("v", lambda v: v.reduce_max(G[:, 12:13], G[:, 16:24], axis=AX.X),
                 waits=[("dv", dvP)], inc=1)
            dvM1 = P2.n("dv")
            P2.op("v", lambda v: v.tensor_scalar(G[:, 24:32], G[:, 16:24],
                                                G[:, 12:13], None, ALU.subtract),
                 waits=[("dv", dvM1)], inc=1)
            dvD = P2.n("dv")
            P2.op("v", lambda v: v.tensor_single_scalar(G[:, 32:40], G[:, 24:32],
                                                       0.0, ALU.is_equal),
                 waits=[("dv", dvD)], inc=1)
            dvEQ = P2.n("dv")
            P2.op("v", lambda v: v.scalar_tensor_tensor(G[:, 24:32], G[:, 32:40],
                                                       -1e30, G[:, 16:24],
                                                       ALU.mult, ALU.add),
                 waits=[("dv", dvEQ)], inc=1)
            dvMK = P2.n("dv")
            P2.op("v", lambda v: v.reduce_max(G[:, 13:14], G[:, 24:32], axis=AX.X),
                 waits=[("dv", dvMK)], inc=1)
            dvM2 = P2.n("dv")
            P2.op("v", lambda v: v.tensor_single_scalar(G[:, 32:40], G[:, 16:24],
                                                       G[:, 13:14], ALU.is_ge),
                 waits=[("dv", dvM2)], inc=1)
            dvKP = P2.n("dv")
            P2.op("v", lambda v: v.tensor_tensor(G[:, 40:48], G[:, 16:24],
                                                G[:, 32:40], ALU.mult),
                 waits=[("dv", dvKP)], inc=1)
            dvMW = P2.n("dv")
            tguard2()
            top2(lambda t: t.transpose(Db[0:8, 128:256], G[:, 40:48], ident[:]),
                waits=[("dv", dvMW)] + bk2["Db"].acquire(), inc=1)
            pe_mw = P2.n("pe")
            P2.op("v", lambda v, i16=i16:
                 v.tensor_copy(m_fm[0:8, i16 * 128:(i16 + 1) * 128],
                               Db[0:8, 128:256]),
                 waits=[("pe", pe_mw), ("dv", dv_mfm_init)], inc=1)
            bk2["Db"].read("dv", P2.n("dv"))
        dv_mfm = P2.n("dv")



        def w12_src(e, hc):
            wt1 = ap["rw1"] if e < 8 else ap["sw1"]
            wt2 = ap["rw2"] if e < 8 else ap["sw2"]
            ei = e % 8
            s1_ = wt1[ei, :, hc * 128:(hc + 1) * 128].rearrange(
                "(dc p) h -> p dc h", p=128)
            s2_ = wt2[ei, :, hc * 128:(hc + 1) * 128].rearrange(
                "(dc p) h -> p dc h", p=128)
            return s1_, s2_

        def w3_src(e, hc):
            wt3 = ap["rw3"] if e < 8 else ap["sw3"]
            return wt3[e % 8, hc * 128:(hc + 1) * 128, :]

        EORDER = [8, 9] + list(range(8))
        wslices = [(e, hc) for e in EORDER for hc in range(16)]
        w12v = lambda buf, par: buf[:, par * 1024:(par + 1) * 1024].rearrange(
            "p (dc h) -> p dc h", dc=DC)

        w12_rdy = {}
        w12_free = [[], []]
        w3_rdy = [0] * 8
        w3_free = [[] for _ in range(8)]
        glu_free = {}
        si_next = [0]

        def prefetch_w12():
            si = si_next[0]
            if si < len(wslices):
                e, hc = wslices[si]
                sA, sB = w12_src(e, hc)
                ld2(w12v(w1b, si % 2), sA, gpq=True, extra=w12_free[si % 2])
                w12_rdy[si] = ld2(w12v(w2b, si % 2), sB, gpq=True)
                w12_free[si % 2] = []
                si_next[0] += 1

        if "ffn" in stages:
            prefetch_w12()
            prefetch_w12()

        st_last = [0]
        st_par = [0, 0]
        res_free = [[], []]
        ln2_done = set()

        def emit_ln2_tile(i16, wv):
            ln2_done.add(i16)
            rpar = i16 % 2
            for dcq in range(2):
                bnm = ("U1a", "U1b")[dcq]
                bank = PSB2[bnm]
                for j in range(4):
                    dc = dcq * 4 + j
                    w = list(wv)
                    if j == 0:
                        w += bk2[bnm].acquire()
                        tguard2()
                    top2(lambda t, bank=bank, j=j, dc=dc, i16=i16:
                         t.transpose(bank[:, j * 128:(j + 1) * 128],
                                     x_fm3[:, dc, i16 * 128:(i16 + 1) * 128],
                                     ident[:]),
                         waits=w, inc=(1 if j == 3 else None))
                pe_tp = P2.n("pe")
                P2.op("v", lambda v, bank=bank, dcq=dcq, rpar=rpar:
                      v.tensor_copy(res_tm[:, rpar * 1024 + dcq * 512:
                                           rpar * 1024 + (dcq + 1) * 512],
                                    bank[:]),
                      waits=[("pe", pe_tp)] + res_free[rpar], inc=1)
                res_free[rpar] = []
                bk2[bnm].read("dv", P2.n("dv"))
            dv_res = P2.n("dv")
            R = res_tm[:, rpar * 1024:(rpar + 1) * 1024]
            P2.op("v", lambda v, R=R: v.reduce_sum(lnc[:, 0:1], R, axis=AX.X),
                  waits=[("dv", dv_res)], inc=1)
            P2.op("s", lambda s, R=R: s.activation(sqt[:], R[:, 0:512],
                                                   AF.Square, bias=0.0, scale=1.0),
                  waits=[("dv", dv_res), ("dv", P2.n("dv"))], inc=1)
            ac_q1 = P2.n("ac")
            P2.op("v", lambda v: v.reduce_sum(lnc[:, 1:2], sqt[:], axis=AX.X),
                  waits=[("ac", ac_q1)], inc=1)
            dv_q1 = P2.n("dv")
            P2.op("s", lambda s, R=R: s.activation(sqt[:], R[:, 512:1024],
                                                   AF.Square, bias=0.0, scale=1.0),
                  waits=[("dv", dv_q1)], inc=1)
            ac_q2 = P2.n("ac")
            P2.op("v", lambda v: v.reduce_sum(lnc[:, 3:4], sqt[:], axis=AX.X),
                  waits=[("ac", ac_q2)], inc=1)
            dv_s1 = P2.n("dv")
            ac_s2 = P2.n("ac")
            P2.op("v", lambda v: v.tensor_scalar_mul(lnc[:, 2:3], lnc[:, 0:1],
                                                     1.0 / D),
                  waits=[("dv", dv_s1), ("ac", ac_s2)], inc=1)
            dv_mm = P2.n("dv")
            P2.op("v", lambda v: v.tensor_tensor(lnc[:, 4:5], lnc[:, 1:2],
                                                 lnc[:, 3:4], ALU.add),
                  waits=[("dv", dv_mm)], inc=1)
            dv_sq = P2.n("dv")
            P2.op("v", lambda v: v.tensor_scalar_mul(lnc[:, 3:4], lnc[:, 4:5],
                                                     1.0 / D),
                  waits=[("dv", dv_sq)], inc=1)
            dv_m = P2.n("dv")
            P2.op("v", lambda v: v.tensor_tensor(lnc[:, 4:5], lnc[:, 2:3],
                                                 lnc[:, 2:3], ALU.mult),
                  waits=[("dv", dv_m)], inc=1)
            dv_m2 = P2.n("dv")
            P2.op("v", lambda v: v.tensor_tensor(lnc[:, 5:6], lnc[:, 3:4],
                                                 lnc[:, 4:5], ALU.subtract),
                  waits=[("dv", dv_m2)], inc=1)
            dv_m3 = P2.n("dv")
            P2.op("v", lambda v: v.tensor_scalar_add(lnc[:, 6:7], lnc[:, 5:6],
                                                     EPS), waits=[("dv", dv_m3)],
                  inc=1)
            dv_v = P2.n("dv")
            P2.op("s", lambda s: s.activation(lnc[:, 7:8], lnc[:, 6:7], AF.Sqrt,
                                              bias=0.0, scale=1.0),
                  waits=[("dv", dv_v)], inc=1)
            ac_sd = P2.n("ac")
            P2.op("v", lambda v: v.reciprocal(lnc[:, 5:6], lnc[:, 7:8]),
                  waits=[("ac", ac_sd)], inc=1)
            dv_r = P2.n("dv")
            O = out_tm[:, rpar * 1024:(rpar + 1) * 1024]
            P2.op("v", lambda v, R=R, O=O: v.tensor_scalar(O, R, lnc[:, 2:3],
                                                      lnc[:, 5:6], ALU.subtract,
                                                      ALU.mult),
                  waits=[("dv", dv_r), ("st", st_par[rpar])], inc=1)
            dv_o = P2.n("dv")
            res_free[rpar] = [("dv", dv_o)]
            P2.op("sync", lambda en, i16=i16, O=O:
                  en.dma_start(out=y[i16 * 128:(i16 + 1) * 128, :], in_=O),
                  waits=[("dv", dv_o)], inc=("st", 16))
            st_last[0] = P2.n("st")
            st_par[rpar] = P2.n("st")


        for ei in range(NE if "ffn" in stages else 0):
            e = EORDER[ei]
            for tt in range(TT):
                tguard2()
                top2(lambda t, e=e, tt=tt:
                     t.matmul(MBp[:], eye10[0:NE, e * 128:(e + 1) * 128],
                              m_fm[0:NE, tt * 512:(tt + 1) * 512],
                              start=True, stop=True),
                     waits=[("ld", n_eye),
                            ("dv", dv_mfm_init if e >= 8 else dv_mfm)]
                     + bk2["MB"].acquire(), inc=1)
                pe_mb = P2.n("pe")
                P2.op("v", lambda v, tt=tt:
                      v.tensor_copy(m_bc[:, tt * 512:(tt + 1) * 512], MBp[:]),
                      waits=[("pe", pe_mb)], inc=1)
                bk2["MB"].read("dv", P2.n("dv"))
            dv_mbc = P2.n("dv")
            tguard2()
            for hh in range(2):
                for hc8 in range(8):
                    hcg = hh * 8 + hc8
                    w3_rdy[hc8] = ld2(w3r3[:, hc8, :], w3_src(e, hcg), gpq=True,
                                      extra=w3_free[hc8])
                    w3_free[hc8] = []
                for hc8 in range(8):
                    hcg = hh * 8 + hc8
                    si = ei * 16 + hcg
                    par = si % 2
                    for tt in range(TT):
                        up = tt % 2
                        u1nm, u2nm = ("U1a", "U2a") if up == 0 else ("U1b", "U2b")
                        U1, U2 = PSB2[u1nm], PSB2[u2nm]
                        for dc in range(DC):
                            w = [("wg", w12_rdy[si])]
                            if dc == 0:
                                w += bk2[u1nm].acquire()
                            top2(lambda t, U1=U1, par=par, dc=dc, tt=tt:
                                 t.matmul(U1[:],
                                          w1b[:, par * 1024 + dc * 128:
                                              par * 1024 + (dc + 1) * 128],
                                          x_bf3[:, dc, tt * 512:(tt + 1) * 512],
                                          start=(dc == 0), stop=(dc == DC - 1)),
                                 waits=w, inc=(1 if dc == DC - 1 else None))
                        for dc in range(DC):
                            w = []
                            if dc == 0:
                                w += bk2[u2nm].acquire()
                            top2(lambda t, U2=U2, par=par, dc=dc, tt=tt:
                                 t.matmul(U2[:],
                                          w2b[:, par * 1024 + dc * 128:
                                              par * 1024 + (dc + 1) * 128],
                                          x_bf3[:, dc, tt * 512:(tt + 1) * 512],
                                          start=(dc == 0), stop=(dc == DC - 1)),
                                 waits=w, inc=(1 if dc == DC - 1 else None))
                        pe_up = P2.n("pe")
                        scr = (fsc1, fsc2)[up]
                        bcol = hcg * 10 + e
                        P2.op("s", lambda s, U1=U1, scr=scr, bcol=bcol:
                              s.activation(scr[:], U1[:], AF.Silu,
                                           bias=b1c[:, bcol:bcol + 1], scale=1.0),
                              waits=[("pe", pe_up), ("dv", P2.n("dv"))], inc=1)
                        ac_si = P2.n("ac")
                        bk2[u1nm].read("ac", ac_si)
                        gfw = [("ac", ac_si)] + glu_free.pop((hc8, tt), [])
                        P2.op("v", lambda v, U2=U2, scr=scr, hc8=hc8, tt=tt,
                              bcol=bcol:
                              v.scalar_tensor_tensor(
                                  glu3[:, hc8, tt * 512:(tt + 1) * 512], U2[:],
                                  b2c[:, bcol:bcol + 1], scr[:],
                                  ALU.add, ALU.mult),
                              waits=gfw, inc=1)
                        bk2[u2nm].read("dv", P2.n("dv"))
                        P2.op("v", lambda v, hc8=hc8, tt=tt:
                              v.tensor_tensor(glu3[:, hc8, tt * 512:(tt + 1) * 512],
                                              glu3[:, hc8, tt * 512:(tt + 1) * 512],
                                              m_bc[:, tt * 512:(tt + 1) * 512],
                                              ALU.mult),
                              waits=[("dv", dv_mbc)], inc=1)
                    w12_free[par] = [("pe", P2.n("pe"))]
                    prefetch_w12()
                dv_glu = P2.n("dv")
                gidx = 0
                for tt in range(TT):
                    for dc in range(DC):
                        dnm = ("Da", "Db")[gidx % 2]
                        Dk = PSB2[dnm]
                        with_bias = (ei == 2 and hh == 0)
                        if with_bias:
                            tguard2()
                            top2(lambda t, Dk=Dk, dc=dc, tt=tt:
                                 t.matmul(Dk[:], b3[0:NE, dc * 128:(dc + 1) * 128],
                                          m_fm[0:NE, tt * 512:(tt + 1) * 512],
                                          start=True, stop=False),
                                 waits=[("dv", dv_glu), ("wg", n_b3)]
                                 + bk2[dnm].acquire())
                            tguard2()
                        for hc8 in range(8):
                            w = [("dv", dv_glu), ("wg", w3_rdy[hc8])]
                            if hc8 == 0 and not with_bias:
                                w += bk2[dnm].acquire()
                            top2(lambda t, Dk=Dk, hc8=hc8, dc=dc, tt=tt,
                                 wb=with_bias:
                                 t.matmul(Dk[:],
                                          w3r3[:, hc8, dc * 128:(dc + 1) * 128],
                                          glu3[:, hc8, tt * 512:(tt + 1) * 512],
                                          start=(hc8 == 0 and not wb),
                                          stop=(hc8 == 7)),
                                 waits=w, inc=(1 if hc8 == 7 else None))
                        pe_dn = P2.n("pe")
                        for hc8 in range(8):
                            glu_free[(hc8, tt)] = [("pe", pe_dn)]
                        P2.op("v", lambda v, Dk=Dk, dc=dc, tt=tt:
                              v.tensor_tensor(x_fm3[:, dc, tt * 512:(tt + 1) * 512],
                                              x_fm3[:, dc, tt * 512:(tt + 1) * 512],
                                              Dk[:], ALU.add),
                              waits=[("pe", pe_dn)], inc=1)
                        bk2[dnm].read("dv", P2.n("dv"))
                        gidx += 1
                    if ei == NE - 1 and hh == 1:
                        dv_tt = P2.n("dv")
                        ac_tt = P2.n("ac")
                        for i16 in range(tt * 4, tt * 4 + 4):
                            emit_ln2_tile(i16, [("dv", dv_tt), ("ac", ac_tt)])
                pe_dn_all = P2.n("pe")
                for hc8 in range(8):
                    w3_free[hc8] = [("pe", pe_dn_all)]
        dv_ffn = P2.n("dv")
        ac_ffn = P2.n("ac")
        for i16 in range(T128):
            if i16 not in ln2_done:
                emit_ln2_tile(i16, [("dv", dv_ffn), ("ac", ac_ffn)])

        if "dbg" in stages:
            dbg1 = nc.dram_tensor("dbg_lnc", [128, 8], F32, kind="ExternalOutput").ap()
            dbg2 = nc.dram_tensor("dbg_sqt", [128, 1024], F32, kind="ExternalOutput").ap()
            dbg3 = nc.dram_tensor("dbg_res", [128, 2048], F32, kind="ExternalOutput").ap()
            for dst, srcb in ((dbg1, lnc[:]), (dbg2, sqt[:]), (dbg3, res_tm[:])):
                P2.op("sync", lambda e, d=dst, s=srcb: e.dma_start(out=d, in_=s),
                      waits=[("dv", P2.n("dv")), ("ac", P2.n("ac"))], inc=("st", 16))
            st_last[0] = P2.n("st")

        P2.op("v", lambda v: v.tensor_scalar_mul(lnc[:, 3:4], lnc[:, 2:3], 1.0),
              waits=[("st", st_last[0])], inc=1)

        with nc.Block() as blk2:
            P2.emit(blk2, sems)
        es2.close()

    return nc


_NC_CACHE = {}

def _get_nc():
    if "nc" not in _NC_CACHE:
        _NC_CACHE["nc"] = build_nc()
    return _NC_CACHE["nc"]


def kernel(**inputs):
    consts = _consts()
    x = np.ascontiguousarray(inputs["L0_attn_x"], dtype=np.float32)
    base = {}
    for k, v in inputs.items():
        if k == "L0_attn_x":
            continue
        nm = k.replace("L0_attn_", "").replace("L0_ffn_", "")
        base[nm] = np.ascontiguousarray(v, dtype=np.float32)
    base.update(consts)
    nc = _get_nc()
    in_maps = []
    for c in range(NCORE):
        m = dict(base)
        m["x"] = np.ascontiguousarray(x[c])
        in_maps.append(m)
    res = run_bass_kernel_spmd(nc, in_maps, core_ids=list(range(NCORE)))
    return np.stack([np.asarray(r["y"], dtype=np.float32) for r in res.results],
                    axis=0)
